# version 06
# baseline (speedup 1.0000x reference)
import numpy as np
import ml_dtypes
from contextlib import ExitStack

import concourse.bass as bass
import concourse.mybir as mybir
from concourse.bass_utils import run_bass_kernel_spmd

F32 = mybir.dt.float32
BF16 = mybir.dt.bfloat16
I32 = mybir.dt.int32
AF = mybir.ActivationFunctionType
ALU = mybir.AluOpType
AX = mybir.AxisListType

NCORES = 8
B = 4
S = 2048
D = 1024
EC = 8
NP = 8
STN = 16
LB = 512
NLB = S // LB
FF = 3072
FC = 24
LN_EPS = 1e-5
RMS_EPS = 1.1920929e-07
NEG_LN8 = -2.0794415416798357
DEBUG_C = False
DEBUG_IO = True
QSCALE = 0.125


class Sem:
    def __init__(self, h):
        self.h = h
        self.n = 0

    def bump(self, k=1):
        self.n += k
        return self.n


class Ctx:
    ENGINES = ("sync", "tensor", "vector", "scalar", "gpsimd")

    def __init__(self, nc):
        self.nc = nc
        self.stack = None
        self.sems = {}
        self.streams = {e: [] for e in self.ENGINES}
        self.nbar = 0

    def make_sems(self, stack):
        for name in ("pe", "act", "dve", "pool", "sd", "sg", "sst", "bar"):
            self.sems[name] = Sem(stack.enter_context(self.nc.semaphore(name)))

    def emit(self, eng, fn):
        self.streams[eng].append(fn)

    def wait(self, eng, sem, val):
        if val is None or val <= 0:
            return
        self.emit(eng, lambda E, s=sem.h, v=val: E.wait_ge(s, v))

    def barrier(self):
        bar = self.sems["bar"]
        self.wait("sync", self.sems["sd"], self.sems["sd"].n)
        self.wait("sync", self.sems["sst"], self.sems["sst"].n)
        self.wait("gpsimd", self.sems["sg"], self.sems["sg"].n)
        for e in self.ENGINES:
            self.emit(e, lambda E, s=bar.h: E.sem_inc(s, 1))
        self.nbar += 1
        tgt = 5 * self.nbar
        bar.n = tgt
        for e in self.ENGINES:
            self.emit(e, lambda E, s=bar.h, v=tgt: E.wait_ge(s, v))

    def replay(self, block):
        for eng in self.ENGINES:
            ops = self.streams[eng]
            if not ops:
                continue
            sec = getattr(block, eng)

            @sec
            def _(E, ops=ops):
                for fn in ops:
                    fn(E)
        self.streams = {e: [] for e in self.ENGINES}


def _inc(instr, sem, k=1):
    instr.then_inc(sem.h, k)


def mm(ctx, out, lhsT, rhs, start, stop, tp=None, inc=False, fence=False, fence_ap=None):
    sem = ctx.sems["pe"]
    if fence:
        ctx.emit("tensor", lambda E, o=out, l=lhsT, r=rhs, st=start, sp=stop, tp=tp:
                 E.matmul(o, l, r, start=st, stop=sp, tile_position=tp))
        n = sem.bump()
        zf = ctx.zf
        if fence_ap is not None:
            fo = fence_ap
        else:
            w = min(256, int(out.shape[1]))
            fo = out[0:1, 0:w]
        fw = int(fo.shape[1])
        ctx.emit("tensor", lambda E, o=fo, z=zf, fw=fw:
                 _inc(E.matmul(o, z[0:1, 0:1], z[0:1, 0:fw], start=False, stop=True), sem))
        return n
    if inc:
        n = sem.bump()
        ctx.emit("tensor", lambda E, o=out, l=lhsT, r=rhs, st=start, sp=stop, tp=tp:
                 _inc(E.matmul(o, l, r, start=st, stop=sp, tile_position=tp), sem))
        return n
    ctx.emit("tensor", lambda E, o=out, l=lhsT, r=rhs, st=start, sp=stop, tp=tp:
             E.matmul(o, l, r, start=st, stop=sp, tile_position=tp))
    return None


def act(ctx, out, in_, func, bias=None, scale=None, inc=False, selfsync=False):
    sem = ctx.sems["act"]
    kw = {}
    if bias is not None:
        kw["bias"] = bias
    if scale is not None:
        kw["scale"] = scale
    if inc or selfsync:
        n = sem.bump()
        ctx.emit("scalar", lambda E, o=out, i=in_, f=func, kw=kw: _inc(E.activation(o, i, f, **kw), sem))
        if selfsync:
            ctx.wait("scalar", sem, n)
        return n
    ctx.emit("scalar", lambda E, o=out, i=in_, f=func, kw=kw: E.activation(o, i, f, **kw))
    return None


def vop(ctx, name, *args, inc=False, selfsync=False, eng="vector", **kw):
    sem = ctx.sems["dve" if eng == "vector" else "pool"]
    if inc or selfsync:
        n = sem.bump()
        ctx.emit(eng, lambda E, a=args, kw=kw: _inc(getattr(E, name)(*a, **kw), sem))
        if selfsync:
            ctx.wait(eng if eng != "vector" else "vector", sem, n)
        return n
    ctx.emit(eng, lambda E, a=args, kw=kw: getattr(E, name)(*a, **kw))
    return None


def dma(ctx, queue, out, in_, sem_name):
    sem = ctx.sems[sem_name]
    n = sem.bump(16)
    ctx.emit(queue, lambda E, o=out, i=in_: _inc(E.dma_start(out=o, in_=i), sem, 16))
    return n



def ln_apply(ctx, PS, evac, yT, sm, ones_cf, ones_r, eps, rms=False, nw=None,
             sum_banks=(2, 3), bc_banks=(4, 5)):
    sem = ctx.sems
    c_ts = None
    if not rms:
        vop(ctx, "tensor_copy", sm["ts"][:], yT[:, 0:LB])
        for ec in range(1, EC):
            c_ts = vop(ctx, "tensor_tensor", sm["ts"][:], sm["ts"][:], yT[:, ec * LB:(ec + 1) * LB],
                       ALU.add, inc=(ec == EC - 1))
    c_tsq = None
    for ec in range(EC):
        vop(ctx, "tensor_tensor", sm["sqc"][:], yT[:, ec * LB:(ec + 1) * LB], yT[:, ec * LB:(ec + 1) * LB], ALU.mult)
        if ec == 0:
            vop(ctx, "tensor_copy", sm["tsq"][:], sm["sqc"][:])
        else:
            c_tsq = vop(ctx, "tensor_tensor", sm["tsq"][:], sm["tsq"][:], sm["sqc"][:],
                        ALU.add, inc=(ec == EC - 1))
    b2, b3 = sum_banks
    for bk in sum_banks:
        ctx.wait("tensor", sem["dve"], evac.get(bk))
        ctx.wait("tensor", sem["act"], evac.get(("act", bk)))
    ctx.wait("tensor", sem["dve"], c_tsq)
    if not rms:
        mm(ctx, PS[0:1, b2 * LB:(b2 + 1) * LB], ones_cf[:, 0:1], sm["ts"][:], start=True, stop=True)
    c_mm = mm(ctx, PS[0:1, b3 * LB:(b3 + 1) * LB], ones_cf[:, 0:1], sm["tsq"][:], start=True, stop=True, fence=True)
    ctx.wait("vector", sem["pe"], c_mm)
    if not rms:
        vop(ctx, "tensor_scalar", sm["m"][:], PS[0:1, b2 * LB:(b2 + 1) * LB], 1.0 / D, None, ALU.mult)
        vop(ctx, "tensor_scalar", sm["var"][:], PS[0:1, b3 * LB:(b3 + 1) * LB], 1.0 / D, None,
            ALU.mult, selfsync=True)
        vop(ctx, "tensor_tensor", sm["sqc"][0:1, 0:LB], sm["m"][:], sm["m"][:], ALU.mult,
            selfsync=True)
        ev23 = vop(ctx, "tensor_tensor", sm["var"][:], sm["var"][:], sm["sqc"][0:1, 0:LB],
                   ALU.subtract, inc=True, selfsync=True)
    else:
        ev23 = vop(ctx, "tensor_scalar", sm["var"][:], PS[0:1, b3 * LB:(b3 + 1) * LB], 1.0 / D, None,
                   ALU.mult, inc=True, selfsync=True)
    evac[b2] = evac[b3] = ev23
    ctx.wait("scalar", sem["dve"], ev23)
    act(ctx, sm["r"][:], sm["var"][:], AF.Ln, bias=eps, selfsync=True)
    c_r = act(ctx, sm["r"][:], sm["r"][:], AF.Exp, scale=-0.5, inc=True, selfsync=True)
    c_mr = None
    if not rms:
        ctx.wait("vector", sem["act"], c_r)
        c_mr = vop(ctx, "tensor_tensor", sm["mr"][:], sm["m"][:], sm["r"][:], ALU.mult, inc=True)
    b4, b5 = bc_banks
    for bk in bc_banks:
        ctx.wait("tensor", sem["dve"], evac.get(bk))
        ctx.wait("tensor", sem["act"], evac.get(("act", bk)))
    ctx.wait("tensor", sem["act"], c_r)
    c_b = mm(ctx, PS[:, b4 * LB:(b4 + 1) * LB], ones_r[0:1, 0:128], sm["r"][:], start=True, stop=True,
             fence=rms)
    if not rms:
        ctx.wait("tensor", sem["dve"], c_mr)
        c_b = mm(ctx, PS[:, b5 * LB:(b5 + 1) * LB], ones_r[0:1, 0:128], sm["mr"][:], start=True, stop=True, fence=True)
    ctx.wait("vector", sem["pe"], c_b)
    last = None
    for ec in range(EC):
        sl = yT[:, ec * LB:(ec + 1) * LB]
        vop(ctx, "tensor_tensor", sl, sl, PS[:, b4 * LB:(b4 + 1) * LB], ALU.mult)
        if not rms:
            last = vop(ctx, "tensor_tensor", sl, sl, PS[:, b5 * LB:(b5 + 1) * LB], ALU.subtract, inc=True)
        else:
            last = vop(ctx, "tensor_scalar", sl, sl, nw[:, ec:ec + 1], None, ALU.mult, inc=True)
        evac[("ln", ec)] = last
    evac[b4] = evac[b5] = last
    return last


def stage_A(ctx, st_, nc, PS, xT, wq2, wk2, wv2, wo2, x1T, x1bf):
    E = st_.enter_context
    sem = ctx.sems

    u0 = E(nc.sbuf_tensor("a_u0", [128, 16384], BF16))
    wbuf = E(nc.sbuf_tensor("a_wbuf", [128, EC * D], BF16))
    wo = E(nc.sbuf_tensor("a_wo", [128, EC * D], BF16))
    qT = E(nc.sbuf_tensor("a_qT", [128, NP * S], BF16))
    kT = E(nc.sbuf_tensor("a_kT", [128, NP * S], BF16))
    vv = E(nc.sbuf_tensor("a_vv", [128, STN * D], BF16))
    yT = E(nc.sbuf_tensor("a_yT", [128, EC * LB], F32))
    xstg = yT
    stgb = E(nc.sbuf_tensor("a_stgb", [128, 2 * LB], BF16))
    accA = E(nc.sbuf_tensor("a_accA", [128, LB], F32))[:]
    accB = E(nc.sbuf_tensor("a_accB", [128, LB], F32))[:]
    bcS = E(nc.sbuf_tensor("a_bcS", [128, LB], F32))[:]
    rr = E(nc.sbuf_tensor("a_rr", [1, 2 * LB], F32))
    ones_c = E(nc.sbuf_tensor("a_ones_c", [128, 1], BF16))
    ones_cf = E(nc.sbuf_tensor("a_ones_cf", [128, 1], F32))
    ones_r = E(nc.sbuf_tensor("a_ones_r", [1, 128], F32))
    qkm = E(nc.sbuf_tensor("a_qkm", [1, 4], F32))
    negB = E(nc.sbuf_tensor("a_negB", [128, 1], F32))
    tmp1 = E(nc.sbuf_tensor("a_tmp1", [1, 1], F32))
    ts_t = E(nc.sbuf_tensor("a_ts", [128, LB], F32))
    tsq_t = E(nc.sbuf_tensor("a_tsq", [128, LB], F32))
    sqc_t = E(nc.sbuf_tensor("a_sqc", [128, LB], F32))
    row4 = E(nc.sbuf_tensor("a_row4", [1, 3 * LB], F32))
    cst = E(nc.sbuf_tensor("a_cst", [1, 2], F32))
    sm = {"ts": ts_t, "tsq": tsq_t, "sqc": sqc_t,
          "m": row4[0:1, 0:LB], "var": row4[0:1, LB:2 * LB],
          "r": row4[0:1, 2 * LB:3 * LB], "mr": row4[0:1, 0:LB]}

    xbT = u0
    eT = u0[:, 0:8 * 1024]
    accb = u0[:, 8192:8192 + 2 * LB]
    concatT = u0[:, 9216:9216 + EC * LB]
    sqt = u0[:, 0:S]

    c_ones = None
    vop(ctx, "memset", ones_c[:], 1.0)
    vop(ctx, "memset", ones_cf[:], 1.0)
    vop(ctx, "memset", cst[0:1, 0:1], LN_EPS)
    vop(ctx, "memset", cst[0:1, 1:2], NEG_LN8)
    c_ones = vop(ctx, "memset", ones_r[:], 1.0, inc=True)
    ctx.wait("scalar", ctx.sems["dve"], c_ones)

    def bank(i, rows=128, w=LB):
        return PS[0:rows, i * LB:i * LB + w]

    evac = {}
    c_free_u0 = None
    c_free_w = None
    stg_dve = {0: None, 1: None}

    def ldcast(dst_slice, src_ap, half, width):
        ctx.wait("sync", sem["dve"], stg_dve[half])
        c_l = dma(ctx, "sync", xstg[:, half * S:half * S + width], src_ap, "sd")
        ctx.wait("vector", sem["sd"], c_l)
        c_c = vop(ctx, "tensor_copy", dst_slice, xstg[:, half * S:half * S + width], inc=True)
        stg_dve[half] = c_c
        return c_c

    for b in range(B):
        ctx.wait("vector", sem["pe"], c_free_u0)
        c_xb = None
        for ec in range(EC):
            c_xb = ldcast(xbT[:, ec * S:(ec + 1) * S], xT[b, ec * 128:(ec + 1) * 128, :], ec % 2, S)
        ctx.wait("vector", sem["pe"], c_free_w)
        c_wq = None
        for ec in range(EC):
            c_wq = ldcast(wbuf[:, ec * D:(ec + 1) * D], wq2[ec * 128:(ec + 1) * 128, :], ec % 2, D)
        if b == 0:
            c_wo = None
            for ec in range(EC):
                c_wo = ldcast(wo[:, ec * D:(ec + 1) * D], wo2[ec * 128:(ec + 1) * 128, :], ec % 2, D)

        ctx.wait("tensor", sem["dve"], c_ones)

        def proj(dst, c_w):
            ctx.wait("tensor", sem["dve"], max(c_xb, c_w))
            last = None
            for p in range(NP):
                for nb in range(NLB):
                    bk = nb
                    ctx.wait("tensor", sem["dve"], evac.get(bk))
                    ctx.wait("tensor", sem["act"], evac.get(("act", bk)))
                    cm = None
                    for ec in range(EC):
                        cm = mm(ctx, bank(bk), wbuf[:, ec * D + p * 128:ec * D + p * 128 + 128],
                                xbT[:, ec * S + nb * LB:ec * S + nb * LB + LB],
                                start=(ec == 0), stop=(ec == EC - 1), fence=(ec == EC - 1))
                    ctx.wait("vector", sem["pe"], cm)
                    evac[bk] = vop(ctx, "tensor_copy", dst[:, p * S + nb * LB:p * S + nb * LB + LB],
                                   bank(bk), inc=True)
            return cm

        c_pe_q = proj(qT, c_wq)
        ctx.wait("vector", sem["pe"], c_pe_q)
        c_wk = None
        for ec in range(EC):
            c_wk = ldcast(wbuf[:, ec * D:(ec + 1) * D], wk2[ec * 128:(ec + 1) * 128, :], ec % 2, D)
        c_pe_k = proj(kT, c_wk)
        ctx.wait("vector", sem["pe"], c_pe_k)
        c_wv = None
        for ec in range(EC):
            c_wv = ldcast(wbuf[:, ec * D:(ec + 1) * D], wv2[ec * 128:(ec + 1) * 128, :], ec % 2, D)

        evac["yT_done"] = c_wv
        ctx.wait("tensor", sem["dve"], c_wv)
        c_pe_v = None
        for st in range(STN):
            for half in range(2):
                ctx.wait("tensor", sem["dve"], evac.get(4 + half))
            cm = None
            for ec in range(EC):
                for half in range(2):
                    cm = mm(ctx, bank(4 + half), xbT[:, ec * S + st * 128:ec * S + st * 128 + 128],
                            wbuf[:, ec * D + half * LB:ec * D + half * LB + LB],
                            start=(ec == 0), stop=(ec == EC - 1), fence=(ec == EC - 1 and half == 1))
            ctx.wait("vector", sem["pe"], cm)
            evac[4] = evac[5] = vop(ctx, "tensor_copy", vv[:, st * D:st * D + D],
                                    PS[:, 4 * LB:4 * LB + D], inc=True)
            c_pe_v = cm

        c_qk = vop(ctx, "memset", qkm[:], 0.0, inc=True)
        for j, src in enumerate((qT, kT)):
            for p in range(NP):
                ctx.wait("vector", sem["pe"], c_pe_v)
                csq = vop(ctx, "tensor_tensor", sqt, src[:, p * S:(p + 1) * S],
                          src[:, p * S:(p + 1) * S], ALU.mult, inc=True)
                for nb in range(NLB):
                    bk = nb % 2
                    ctx.wait("tensor", sem["dve"], evac.get(bk))
                    ctx.wait("tensor", sem["dve"], csq)
                    cm = mm(ctx, bank(bk, rows=1), ones_c[:, 0:1], sqt[:, nb * LB:(nb + 1) * LB],
                            start=True, stop=True, fence=True)
                    ctx.wait("vector", sem["pe"], cm)
                    vop(ctx, "tensor_reduce", tmp1[:], bank(bk, rows=1), AX.X, ALU.max,
                        selfsync=True)
                    c_qk = vop(ctx, "tensor_tensor", qkm[0:1, j:j + 1], qkm[0:1, j:j + 1],
                               tmp1[:], ALU.max, inc=True, selfsync=True)
                    evac[bk] = c_qk
        ctx.wait("scalar", sem["dve"], c_qk)
        act(ctx, qkm[0:1, 2:3], qkm[0:1, 0:1], AF.Ln)
        c_l = act(ctx, qkm[0:1, 3:4], qkm[0:1, 1:2], AF.Ln, inc=True)
        ctx.wait("vector", sem["act"], c_l)
        c_s = vop(ctx, "tensor_tensor", tmp1[:], qkm[0:1, 2:3], qkm[0:1, 3:4], ALU.add,
                  inc=True, selfsync=True)
        ctx.wait("scalar", sem["dve"], c_s)
        c_b = act(ctx, tmp1[:], tmp1[:], AF.Exp, scale=0.5, bias=cst[0:1, 1:2], inc=True)
        ctx.wait("vector", sem["act"], c_b)
        c_nb = vop(ctx, "tensor_scalar", tmp1[:], tmp1[:], -1.0, None, ALU.mult,
                   inc=True, selfsync=True)
        ctx.wait("tensor", sem["dve"], c_nb)
        bk = 2
        ctx.wait("tensor", sem["dve"], evac.get(bk))
        cm = mm(ctx, bank(bk, w=1), ones_r[0:1, 0:128], tmp1[:], start=True, stop=True, fence=True,
                fence_ap=bank(bk, rows=1, w=256))
        ctx.wait("vector", sem["pe"], cm)
        evac[bk] = c_negB = vop(ctx, "tensor_copy", negB[:], bank(bk, w=1), inc=True)
        ctx.wait("scalar", sem["dve"], c_negB)

        for lb in range(NLB):
            c_xr = None
            ctx.wait("sync", sem["dve"], evac.get("yT_done"))
            for ec in range(EC):
                c_xr = dma(ctx, "sync", yT[:, ec * LB:(ec + 1) * LB],
                           xT[b, ec * 128:(ec + 1) * 128, lb * LB:(lb + 1) * LB], "sd")
            for p in range(NP):
                cb = 4 + (p % 2)
                ctx.wait("tensor", sem["dve"], evac.get(cb))
                cav = None
                cp = None
                for rnd in range(2):
                    exp_done = []
                    for s8 in range(8):
                        st = rnd * 8 + s8
                        sb = (s8 % 2) * 2
                        for bkx in (sb, sb + 1):
                            ctx.wait("tensor", sem["dve"], evac.get(bkx))
                            ctx.wait("tensor", sem["act"], evac.get(("act", bkx)))
                            ctx.wait("tensor", sem["pool"], evac.get(("pool", bkx)))
                        mm(ctx, PS[:, sb * LB:sb * LB + LB],
                           kT[0:64, p * S + st * 128:p * S + st * 128 + 128],
                           qT[0:64, p * S + lb * LB:p * S + lb * LB + LB],
                           start=True, stop=True, tp=(0, 0))
                        cm = mm(ctx, PS[:, sb * LB + LB:sb * LB + 2 * LB],
                                kT[64:128, p * S + st * 128:p * S + st * 128 + 128],
                                qT[64:128, p * S + lb * LB:p * S + lb * LB + LB],
                                start=True, stop=True, tp=(64, 0), fence=True)
                        evac[sb] = evac[sb + 1] = None
                        ctx.wait("scalar", sem["pe"], cm)
                        ce = act(ctx, eT[:, s8 * 1024:(s8 + 1) * 1024], PS[:, sb * LB:sb * LB + 2 * LB],
                                 AF.Exp, bias=negB[:, 0:1], scale=QSCALE, inc=True)
                        evac[("act", sb)] = evac[("act", sb + 1)] = ce
                        exp_done.append(ce)
                        ctx.wait("gpsimd", sem["act"], ce)
                        if st == 0:
                            vop(ctx, "tensor_copy", accA, eT[:, 0:LB], eng="gpsimd")
                            cp = vop(ctx, "tensor_copy", accB, eT[:, LB:2 * LB], eng="gpsimd", inc=True)
                        else:
                            vop(ctx, "tensor_tensor", accA, accA,
                                eT[:, s8 * 1024:s8 * 1024 + LB], ALU.add, eng="gpsimd")
                            cp = vop(ctx, "tensor_tensor", accB, accB,
                                     eT[:, s8 * 1024 + LB:s8 * 1024 + 2 * LB], ALU.add, eng="gpsimd", inc=True)
                    for s8 in range(8):
                        st = rnd * 8 + s8
                        ctx.wait("tensor", sem["act"], exp_done[s8])
                        mm(ctx, PS[0:64, cb * LB:(cb + 1) * LB],
                           vv[:, st * D + (2 * p) * 64:st * D + (2 * p) * 64 + 64],
                           eT[:, s8 * 1024:s8 * 1024 + LB],
                           start=(st == 0), stop=(st == STN - 1), tp=(0, 0))
                        cav = mm(ctx, PS[64:128, cb * LB:(cb + 1) * LB],
                                 vv[:, st * D + (2 * p + 1) * 64:st * D + (2 * p + 1) * 64 + 64],
                                 eT[:, s8 * 1024 + LB:s8 * 1024 + 2 * LB],
                                 start=(st == 0), stop=(st == STN - 1), tp=(0, 64),
                                 fence=(st == STN - 1))
                vop(ctx, "tensor_copy", accb[0:128, 0:LB], accA, eng="gpsimd")
                c_accb = vop(ctx, "tensor_copy", accb[0:128, LB:2 * LB], accB, eng="gpsimd", inc=True)
                ctx.wait("tensor", sem["pool"], c_accb)
                for bkx in (6, 7):
                    ctx.wait("tensor", sem["dve"], evac.get(bkx))
                mm(ctx, PS[0:1, 6 * LB:7 * LB], ones_c[:, 0:1], accb[0:128, 0:LB], start=True, stop=True)
                cdn = mm(ctx, PS[0:1, 7 * LB:8 * LB], ones_c[:, 0:1], accb[0:128, LB:2 * LB],
                         start=True, stop=True, fence=True)
                ctx.wait("vector", sem["pe"], cdn)
                vop(ctx, "reciprocal", rr[0:1, 0:LB], PS[0:1, 6 * LB:7 * LB])
                c_rr = vop(ctx, "reciprocal", rr[0:1, LB:2 * LB], PS[0:1, 7 * LB:8 * LB], inc=True)
                evac[6] = evac[7] = c_rr
                bb = 0
                ctx.wait("tensor", sem["act"], evac.get(("act", bb)))
                ctx.wait("tensor", sem["pool"], evac.get(("pool", bb)))
                ctx.wait("tensor", sem["dve"], c_rr)
                mm(ctx, PS[0:64, bb * LB:bb * LB + LB], ones_r[0:1, 0:64], rr[0:1, 0:LB],
                   start=True, stop=True, tp=(0, 0))
                cbc = mm(ctx, PS[64:128, bb * LB:bb * LB + LB], ones_r[0:1, 0:64], rr[0:1, LB:2 * LB],
                         start=True, stop=True, tp=(0, 64), fence=True)
                ctx.wait("vector", sem["pe"], cbc)
                cbs = vop(ctx, "tensor_copy", bcS, PS[:, bb * LB:bb * LB + LB], inc=True)
                evac[bb] = cbs
                evac[("act", bb)] = None
                ctx.wait("vector", sem["pe"], cav)
                evac[cb] = vop(ctx, "tensor_tensor", concatT[:, p * LB:(p + 1) * LB],
                               PS[:, cb * LB:(cb + 1) * LB], bcS, ALU.mult, inc=True)
            c_concat = evac[4 + ((NP - 1) % 2)]

            ctx.wait("tensor", sem["dve"], c_concat)
            for ot in range(EC):
                wb = ot % 2
                ctx.wait("tensor", sem["dve"], evac.get(wb))
                ctx.wait("tensor", sem["act"], evac.get(("act", wb)))
                cm = None
                for ec in range(EC):
                    cm = mm(ctx, bank(wb), wo[:, ec * D + ot * 128:ec * D + ot * 128 + 128],
                            concatT[:, ec * LB:(ec + 1) * LB], start=(ec == 0), stop=(ec == EC - 1),
                            fence=(ec == EC - 1))
                ctx.wait("vector", sem["pe"], cm)
                if ot == 0:
                    ctx.wait("vector", sem["sd"], c_xr)
                evac[wb] = vop(ctx, "tensor_tensor", yT[:, ot * LB:(ot + 1) * LB], bank(wb),
                               yT[:, ot * LB:(ot + 1) * LB], ALU.add, inc=True)

            ln_apply(ctx, PS, evac, yT, sm, ones_cf, ones_r, cst[0:1, 0:1])
            c_store = None
            for ec in range(EC):
                hb_ = ec % 2
                ctx.wait("sync", sem["dve"], evac[("ln", ec)])
                c_store = dma(ctx, "sync", x1T[b, ec * 128:(ec + 1) * 128, lb * LB:(lb + 1) * LB],
                              yT[:, ec * LB:(ec + 1) * LB], "sst")
                ctx.wait("vector", sem["sst"], evac.get(("mir", hb_)))
                c_mc = vop(ctx, "tensor_copy", stgb[:, hb_ * LB:(hb_ + 1) * LB],
                           yT[:, ec * LB:(ec + 1) * LB], inc=True)
                ctx.wait("sync", sem["dve"], c_mc)
                evac[("mir", hb_)] = dma(ctx, "sync",
                                         x1bf[b, ec * 128:(ec + 1) * 128, lb * LB:(lb + 1) * LB],
                                         stgb[:, hb_ * LB:(hb_ + 1) * LB], "sst")
            evac["yT_done"] = evac[("ln", EC - 1)]
            ctx.wait("sync", sem["sst"], evac[("mir", (EC - 1) % 2)])

        c_free_u0 = sem["pe"].n
        c_free_w = c_pe_v



def alloc_ln(nc, E, pfx):
    ts_t = E(nc.sbuf_tensor(pfx + "_ts", [128, LB], F32))
    tsq_t = E(nc.sbuf_tensor(pfx + "_tsq", [128, LB], F32))
    sqc_t = E(nc.sbuf_tensor(pfx + "_sqc", [128, LB], F32))
    row4 = E(nc.sbuf_tensor(pfx + "_row4", [1, 3 * LB], F32))
    ones_cf = E(nc.sbuf_tensor(pfx + "_ocf", [128, 1], F32))
    ones_r = E(nc.sbuf_tensor(pfx + "_or", [1, 128], F32))
    cst = E(nc.sbuf_tensor(pfx + "_cst", [1, 2], F32))
    sm = {"ts": ts_t, "tsq": tsq_t, "sqc": sqc_t,
          "m": row4[0:1, 0:LB], "var": row4[0:1, LB:2 * LB],
          "r": row4[0:1, 2 * LB:3 * LB], "mr": row4[0:1, 0:LB]}
    return sm, ones_cf, ones_r, cst


def ln_const_init(ctx, ones_cf, ones_r, cst, eps):
    vop(ctx, "memset", ones_cf[:], 1.0)
    vop(ctx, "memset", cst[0:1, 0:1], eps)
    vop(ctx, "memset", cst[0:1, 1:2], NEG_LN8)
    c = vop(ctx, "memset", ones_r[:], 1.0, inc=True)
    ctx.wait("tensor", ctx.sems["dve"], c)
    ctx.wait("scalar", ctx.sems["dve"], c)
    return c


def stage_B(ctx, st_, nc, PS, x1T, x1bf, wgT, wuT, wdH, wdbf, x2T):
    sem = ctx.sems
    INV127 = 1.0 / 127.0
    INVSQ = 1.0 / (127.0 * 127.0)
    E = st_.enter_context
    wgt = E(nc.sbuf_tensor("b_wgt", [128, EC * FF], BF16))
    wut = E(nc.sbuf_tensor("b_wut", [128, EC * FF], BF16))
    x1b = E(nc.sbuf_tensor("b_x1b", [128, 2 * EC * LB], BF16))
    hbm = E(nc.sbuf_tensor("b_hbm", [128, FC * LB], BF16))
    wdsl = E(nc.sbuf_tensor("b_wdsl", [128, 2 * FC * 128], BF16))
    sgg = E(nc.sbuf_tensor("b_sgg", [128, LB], F32))
    yT = E(nc.sbuf_tensor("b_yT", [128, EC * LB], F32))
    bstg = E(nc.sbuf_tensor("b_bstg", [128, FF], I32))
    sm, ones_cf, ones_r, cst = alloc_ln(nc, E, "b")
    ln_const_init(ctx, ones_cf, ones_r, cst, LN_EPS)
    evac = {}
    c_prev = None
    c_wgu = None
    for j, (dst, src) in enumerate(((wgt, wgT), (wut, wuT))):
        for ec in range(EC):
            ctx.wait("sync", sem["dve"], c_prev)
            c_l = dma(ctx, "sync", bstg[:], src[ec * 128:(ec + 1) * 128, :], "sd")
            ctx.wait("vector", sem["sd"], c_l)
            c_prev = c_wgu = vop(ctx, "tensor_copy", dst[:, ec * FF:(ec + 1) * FF], bstg[:], inc=True)
    c_wdbf = None
    for dt_ in range(EC):
        ctx.wait("sync", sem["dve"], c_prev)
        c_l = dma(ctx, "sync", bstg[:, 0:FC * 128], wdH[dt_], "sd")
        ctx.wait("vector", sem["sd"], c_l)
        ctx.wait("vector", sem["sst"], c_wdbf)
        c_prev = vop(ctx, "tensor_copy", wdsl[:, 0:FC * 128], bstg[:, 0:FC * 128], inc=True)
        ctx.wait("sync", sem["dve"], c_prev)
        c_wdbf = dma(ctx, "sync", wdbf[dt_], wdsl[:, 0:FC * 128], "sst")
    ctx.wait("sync", sem["sst"], c_wdbf)
    c_xr0 = None
    for ec in range(EC):
        c_xr0 = dma(ctx, "sync", yT[:, ec * LB:(ec + 1) * LB],
                    x1T[0, ec * 128:(ec + 1) * 128, 0:LB], "sd")
    evac["c_xr"] = c_xr0
    it = 0
    for b in range(B):
        for lb in range(NLB):
            xp = it % 2
            xbase = xp * EC * LB
            ctx.wait("sync", sem["pe"], evac.get(("x1b", xp)))
            c_x = None
            for ec in range(EC):
                c_x = dma(ctx, "sync", x1b[:, xbase + ec * LB:xbase + (ec + 1) * LB],
                          x1bf[b, ec * 128:(ec + 1) * 128, lb * LB:(lb + 1) * LB], "sd")
            ctx.wait("sync", sem["pe"], evac.get(("wdsl", 0)))
            c_wd = dma(ctx, "sync", wdsl[:, 0:FC * 128], wdbf[0], "sd")
            ctx.wait("tensor", sem["sd"], c_x)
            if it == 0:
                ctx.wait("tensor", sem["dve"], c_wgu)
            cu = None
            for ht in range(FC):
                gb = ht % 2
                ub = 2 + ht % 2
                ctx.wait("tensor", sem["act"], evac.get(("act", gb)))
                ctx.wait("tensor", sem["dve"], evac.get(("dve", ub)))
                ctx.wait("tensor", sem["dve"], evac.get(ub))
                cg = None
                for ec in range(EC):
                    cg = mm(ctx, PS[:, gb * LB:(gb + 1) * LB],
                            wgt[:, ec * FF + ht * 128:ec * FF + ht * 128 + 128],
                            x1b[:, xbase + ec * LB:xbase + (ec + 1) * LB],
                            start=(ec == 0), stop=(ec == EC - 1), fence=(ec == EC - 1))
                for ec in range(EC):
                    cu = mm(ctx, PS[:, ub * LB:(ub + 1) * LB],
                            wut[:, ec * FF + ht * 128:ec * FF + ht * 128 + 128],
                            x1b[:, xbase + ec * LB:xbase + (ec + 1) * LB],
                            start=(ec == 0), stop=(ec == EC - 1), fence=(ec == EC - 1))
                ctx.wait("scalar", sem["pe"], cg)
                ctx.wait("scalar", sem["dve"], evac.get("sgg"))
                c_silu = act(ctx, sgg[:], PS[:, gb * LB:(gb + 1) * LB], AF.Silu,
                             scale=INV127, inc=True)
                evac[("act", gb)] = c_silu
                ctx.wait("vector", sem["act"], c_silu)
                ctx.wait("vector", sem["pe"], cu)
                c_h = vop(ctx, "tensor_tensor", hbm[:, ht * LB:(ht + 1) * LB], sgg[:],
                          PS[:, ub * LB:(ub + 1) * LB], ALU.mult, inc=True)
                evac[("dve", ub)] = c_h
                evac["sgg"] = c_h
            evac[("x1b", xp)] = cu
            c_hall = evac["sgg"]
            ctx.wait("tensor", sem["dve"], c_hall)
            cm = None
            for dt_ in range(EC):
                wdp = dt_ % 2
                wdbase = wdp * FC * 128
                ctx.wait("tensor", sem["sd"], c_wd)
                if dt_ < EC - 1:
                    nwdp = (dt_ + 1) % 2
                    ctx.wait("sync", sem["pe"], evac.get(("wdsl", nwdp)))
                    c_wd = dma(ctx, "sync",
                               wdsl[:, nwdp * FC * 128:(nwdp + 1) * FC * 128],
                               wdbf[dt_ + 1], "sd")
                wb = 4 + dt_ % 2
                ctx.wait("tensor", sem["dve"], evac.get(wb))
                for hc in range(FC):
                    cm = mm(ctx, PS[:, wb * LB:(wb + 1) * LB],
                            wdsl[:, wdbase + hc * 128:wdbase + (hc + 1) * 128],
                            hbm[:, hc * LB:(hc + 1) * LB],
                            start=(hc == 0), stop=(hc == FC - 1), fence=(hc == FC - 1))
                evac[("wdsl", wdp)] = cm
                ctx.wait("vector", sem["pe"], cm)
                if dt_ == 0:
                    ctx.wait("vector", sem["sd"], evac.get("c_xr"))
                evac[wb] = vop(ctx, "scalar_tensor_tensor", yT[:, dt_ * LB:(dt_ + 1) * LB],
                               PS[:, wb * LB:(wb + 1) * LB], INVSQ,
                               yT[:, dt_ * LB:(dt_ + 1) * LB], ALU.mult, ALU.add, inc=True)
            evac[2] = evac.get(("dve", 2))
            evac[3] = evac.get(("dve", 3))
            ln_apply(ctx, PS, evac, yT, sm, ones_cf, ones_r, cst[0:1, 0:1],
                     sum_banks=(2, 3), bc_banks=(6, 7))
            c_st = None
            for ec in range(EC):
                ctx.wait("sync", sem["dve"], evac[("ln", ec)])
                c_st = dma(ctx, "sync", x2T[b, ec * 128:(ec + 1) * 128, lb * LB:(lb + 1) * LB],
                           yT[:, ec * LB:(ec + 1) * LB], "sst")
            ctx.wait("sync", sem["sst"], c_st)
            it += 1
            if it < B * NLB:
                nb2, nlb = divmod(it, NLB)
                c_xr = None
                for ec in range(EC):
                    c_xr = dma(ctx, "sync", yT[:, ec * LB:(ec + 1) * LB],
                               x1T[nb2, ec * 128:(ec + 1) * 128, nlb * LB:(nlb + 1) * LB], "sd")
                evac["c_xr"] = c_xr



def stage_C(ctx, st_, nc, PS, x2T, owT, cparams, x3T, x3bf):
    sem = ctx.sems
    gw7h, cw7h, gb8h, cb8h, ld8h, ob8h, nw8h = cparams
    E = st_.enter_context
    XW = S + 8
    owt = E(nc.sbuf_tensor("c_owt", [128, EC * D], BF16))
    xc = E(nc.sbuf_tensor("c_xc", [128, 2 * XW], F32))
    ga = E(nc.sbuf_tensor("c_ga", [128, S], F32))
    gs = E(nc.sbuf_tensor("c_gs", [128, S], F32))
    ct = E(nc.sbuf_tensor("c_ct", [128, S], F32))
    tp = E(nc.sbuf_tensor("c_tp", [128, 2 * S], F32))
    db = E(nc.sbuf_tensor("c_db", [128, S], F32))
    hch = E(nc.sbuf_tensor("c_hch", [128, S], F32))
    hb = E(nc.sbuf_tensor("c_hb", [128, 2 * EC * S], BF16))
    yC = E(nc.sbuf_tensor("c_yC", [128, EC * LB], F32))
    xr2 = E(nc.sbuf_tensor("c_xr2", [128, EC * LB], F32))
    gw7 = E(nc.sbuf_tensor("c_gw7", [128, 56], F32))
    cw7 = E(nc.sbuf_tensor("c_cw7", [128, 56], F32))
    gb8 = E(nc.sbuf_tensor("c_gb8", [128, 8], F32))
    cb8 = E(nc.sbuf_tensor("c_cb8", [128, 8], F32))
    ld8 = E(nc.sbuf_tensor("c_ld8", [128, 8], F32))
    ob8 = E(nc.sbuf_tensor("c_ob8", [128, 8], F32))
    nw8 = E(nc.sbuf_tensor("c_nw8", [128, 8], F32))
    dec8 = E(nc.sbuf_tensor("c_dec8", [128, 8], F32))
    sm, ones_cf, ones_r, cst = alloc_ln(nc, E, "c")
    cstR = E(nc.sbuf_tensor("c_cstR", [1, 1], F32))
    ln_const_init(ctx, ones_cf, ones_r, cst, LN_EPS)
    vop(ctx, "memset", cstR[:], RMS_EPS)

    stgb3 = E(nc.sbuf_tensor("c_stgb3", [128, 2 * LB], BF16))
    c_ow = None
    for ec in range(EC):
        ctx.wait("sync", sem["dve"], c_ow)
        c_l = dma(ctx, "sync", xc[:, 0:D], owT[ec * 128:(ec + 1) * 128, :], "sd")
        ctx.wait("vector", sem["sd"], c_l)
        c_ow = vop(ctx, "tensor_copy", owt[:, ec * D:(ec + 1) * D], xc[:, 0:D], inc=True)
    evac_ow = c_ow
    c_sm = None
    for dst, src in ((gw7, gw7h), (cw7, cw7h), (gb8, gb8h), (cb8, cb8h),
                     (ld8, ld8h), (ob8, ob8h), (nw8, nw8h)):
        c_sm = dma(ctx, "sync", dst[:], src, "sd")
    ctx.wait("scalar", sem["sd"], c_sm)
    c_dec = act(ctx, dec8[:], ld8[:], AF.Exp, inc=True)

    vop(ctx, "memset", xc[:, 0:3], 0.0)
    vop(ctx, "memset", xc[:, S + 3:XW], 0.0)
    vop(ctx, "memset", xc[:, XW:XW + 3], 0.0)
    c_pads = vop(ctx, "memset", xc[:, XW + S + 3:2 * XW], 0.0, inc=True)
    ctx.wait("scalar", sem["dve"], c_pads)

    evac = {}
    evac["xcd_init"] = c_pads

    def conv_chunk(b, ec):
        hp = b % 2
        xp = (b * EC + ec) % 2
        xb_ = xp * XW
        ctx.wait("sync", sem["dve"], evac.get(("xcd", xp)))
        ctx.wait("sync", sem["dve"], evac.get("xcd_init"))
        ctx.wait("sync", sem["act"], evac.get(("xca", xp)))
        c_x = dma(ctx, "sync", xc[:, xb_ + 3:xb_ + 3 + S], x2T[b, ec * 128:(ec + 1) * 128, :], "sd")
        ctx.wait("scalar", sem["sd"], c_x)
        ctx.wait("scalar", sem["dve"], evac.get("gated"))
        ctx.wait("scalar", sem["act"], c_dec)
        cp = None
        for k in range(7):
            tph = k % 2
            ctx.wait("scalar", sem["pool"], evac.get(("tp", tph)))
            ck = act(ctx, tp[:, tph * S:(tph + 1) * S], xc[:, xb_ + k:xb_ + k + S],
                     AF.Identity, scale=cw7[:, ec * 7 + k:ec * 7 + k + 1], inc=True)
            if k == 6:
                evac[("xca", xp)] = ck
            ctx.wait("gpsimd", sem["act"], ck)
            if k == 0:
                ctx.wait("gpsimd", sem["act"], evac.get("tanh"))
                cp = vop(ctx, "tensor_copy", ct[:], tp[:, 0:S], eng="gpsimd", inc=True)
            else:
                cp = vop(ctx, "tensor_tensor", ct[:], ct[:], tp[:, tph * S:tph * S + S],
                         ALU.add, eng="gpsimd", inc=True)
            evac[("tp", tph)] = cp
        ctx.wait("vector", sem["sd"], c_x)
        ctx.wait("vector", sem["act"], evac.get("sig"))
        vop(ctx, "tensor_scalar", ga[:], xc[:, xb_:xb_ + S], gw7[:, ec * 7:ec * 7 + 1], None, ALU.mult)
        cga = None
        for k in range(1, 7):
            cga = vop(ctx, "scalar_tensor_tensor", ga[:], xc[:, xb_ + k:xb_ + k + S],
                      gw7[:, ec * 7 + k:ec * 7 + k + 1], ga[:], ALU.mult, ALU.add,
                      inc=(k == 6))
        evac[("xcd", xp)] = cga
        ctx.wait("scalar", sem["dve"], cga)
        ctx.wait("scalar", sem["dve"], evac.get("gated"))
        c_sg = act(ctx, gs[:], ga[:], AF.Sigmoid, bias=gb8[:, ec:ec + 1], inc=True)
        evac["sig"] = c_sg
        ctx.wait("scalar", sem["pool"], cp)
        c_th = act(ctx, tp[:, 0:S], ct[:], AF.Tanh, bias=cb8[:, ec:ec + 1], inc=True)
        evac["tanh"] = c_th
        ctx.wait("scalar", sem["dve"], evac.get("scan"))
        c_db = act(ctx, db[:], gs[:], AF.Relu, bias=dec8[:, ec:ec + 1], scale=0.0, inc=True)
        ctx.wait("vector", sem["act"], c_th)
        c_gt = vop(ctx, "tensor_tensor", ga[:], gs[:], tp[:, 0:S], ALU.mult, inc=True)
        evac["gated"] = c_gt
        ctx.wait("vector", sem["act"], c_db)
        ctx.wait("vector", sem["pe"], evac.get(("hbfree", hp)))
        c_scan = vop(ctx, "tensor_tensor_scan", hch[:], db[:], ga[:], 0.0, ALU.mult, ALU.add,
                     inc=True)
        evac["scan"] = c_scan
        c_hb = vop(ctx, "tensor_copy", hb[:, hp * EC * S + ec * S:hp * EC * S + (ec + 1) * S],
                   hch[:], inc=True)
        evac[("hbready", hp, ec)] = c_hb

    def norm_lb(b, lb):
        hp = b % 2
        hbase = hp * EC * S
        ctx.wait("sync", sem["dve"], evac.get("xr2"))
        c_y = None
        for ec in range(EC):
            c_y = dma(ctx, "sync", xr2[:, ec * LB:(ec + 1) * LB],
                      x2T[b, ec * 128:(ec + 1) * 128, lb * LB:(lb + 1) * LB], "sd")
        ctx.wait("tensor", sem["dve"], evac.get(("hbready", hp, EC - 1)))
        cm = None
        for et in range(EC):
            wb = et % 2
            ctx.wait("tensor", sem["dve"], evac.get(wb))
            if evac.get("c_ow_waited") is None:
                ctx.wait("tensor", sem["dve"], evac_ow)
                evac["c_ow_waited"] = 1
            for dc in range(EC):
                cm = mm(ctx, PS[:, wb * LB:(wb + 1) * LB],
                        owt[:, dc * D + et * 128:dc * D + et * 128 + 128],
                        hb[:, hbase + dc * S + lb * LB:hbase + dc * S + lb * LB + LB],
                        start=(dc == 0), stop=(dc == EC - 1), fence=(dc == EC - 1))
            ctx.wait("vector", sem["pe"], cm)
            if et == 0:
                ctx.wait("vector", sem["sd"], c_y)
                ctx.wait("vector", sem["sst"], evac.get("yCst"))
            evac[wb] = vop(ctx, "scalar_tensor_tensor", yC[:, et * LB:(et + 1) * LB],
                           PS[:, wb * LB:(wb + 1) * LB], ob8[:, et:et + 1],
                           xr2[:, et * LB:(et + 1) * LB], ALU.add, ALU.add, inc=True)
        evac[("norm_mm", b)] = cm
        ln_apply(ctx, PS, evac, yC, sm, ones_cf, ones_r, cstR[0:1, 0:1], rms=True, nw=nw8)
        last = None
        for ec in range(EC):
            last = vop(ctx, "tensor_tensor", yC[:, ec * LB:(ec + 1) * LB],
                       yC[:, ec * LB:(ec + 1) * LB], xr2[:, ec * LB:(ec + 1) * LB],
                       ALU.add, inc=(ec == EC - 1))
        evac["xr2"] = last
        ln_apply(ctx, PS, evac, yC, sm, ones_cf, ones_r, cst[0:1, 0:1])
        c_st = None
        for ec in range(EC):
            hb2 = ec % 2
            ctx.wait("sync", sem["dve"], evac[("ln", ec)])
            c_st = dma(ctx, "sync", x3T[b, ec * 128:(ec + 1) * 128, lb * LB:(lb + 1) * LB],
                       yC[:, ec * LB:(ec + 1) * LB], "sst")
            ctx.wait("vector", sem["sst"], evac.get(("mir3", hb2)))
            c_mc = vop(ctx, "tensor_copy", stgb3[:, hb2 * LB:(hb2 + 1) * LB],
                       yC[:, ec * LB:(ec + 1) * LB], inc=True)
            ctx.wait("sync", sem["dve"], c_mc)
            c_st = dma(ctx, "sync", x3bf[b, ec * 128:(ec + 1) * 128, lb * LB:(lb + 1) * LB],
                       stgb3[:, hb2 * LB:(hb2 + 1) * LB], "sst")
            evac[("mir3", hb2)] = c_st
        evac["yCst"] = c_st

    for ec in range(EC):
        conv_chunk(0, ec)
    for b in range(B):
        for lb in range(NLB):
            norm_lb(b, lb)
            if b + 1 < B:
                conv_chunk(b + 1, 2 * lb)
                conv_chunk(b + 1, 2 * lb + 1)
        evac[("hbfree", b % 2)] = evac[("norm_mm", b)]


def stage_D(ctx, st_, nc, PS, x3T, x3bf, wlT, alphah, blh, outT):
    sem = ctx.sems
    E = st_.enter_context
    wl = E(nc.sbuf_tensor("d_wl", [128, EC * D], F32))
    w3 = E(nc.sbuf_tensor("d_w3", [128, EC * D], BF16))
    gtile = E(nc.sbuf_tensor("d_g", [128, D], F32))
    ltile = E(nc.sbuf_tensor("d_l", [128, D], F32))
    wabs = E(nc.sbuf_tensor("d_wabs", [128, 8], F32))
    delt = E(nc.sbuf_tensor("d_delt", [128, 1], F32))
    ndelt = E(nc.sbuf_tensor("d_ndelt", [128, 1], F32))
    alph = E(nc.sbuf_tensor("d_alph", [128, 1], F32))
    alps = E(nc.sbuf_tensor("d_alps", [1, 1], F32))
    bl = E(nc.sbuf_tensor("d_bl", [128, 8], F32))
    x3b = E(nc.sbuf_tensor("d_x3b", [128, 2 * EC * LB], BF16))
    yT = E(nc.sbuf_tensor("d_yT", [128, EC * LB], F32))
    tmpc = E(nc.sbuf_tensor("d_tmpc", [128, LB], F32))
    sm, ones_cf, ones_r, cst = alloc_ln(nc, E, "d")
    ln_const_init(ctx, ones_cf, ones_r, cst, LN_EPS)

    c_wl = dma(ctx, "sync", wl[:].rearrange("p (c n) -> p c n", n=D),
               wlT.rearrange("(c p) n -> p c n", p=128), "sd")
    dma(ctx, "sync", alps[:], alphah, "sd")
    c_bl = dma(ctx, "sync", bl[:], blh, "sd")
    ctx.wait("vector", sem["sd"], c_wl)
    c_ab = None
    for ec in range(EC):
        c_ab = vop(ctx, "tensor_reduce", wabs[:, ec:ec + 1], wl[:, ec * D:(ec + 1) * D],
                   AX.X, ALU.add, inc=(ec == EC - 1), apply_absolute_value=True)
    ctx.wait("tensor", sem["dve"], c_ab)
    c_mm = mm(ctx, PS[0:1, 0:8], ones_cf[:, 0:1], wabs[:], start=True, stop=True, fence=True,
              fence_ap=PS[0:1, 0:256])
    ctx.wait("vector", sem["pe"], c_mm)
    vop(ctx, "tensor_reduce", sm["m"][0:1, 0:1], PS[0:1, 0:8], AX.X, ALU.add, selfsync=True)
    c_d = vop(ctx, "tensor_scalar", sm["m"][0:1, 0:1], sm["m"][0:1, 0:1],
              0.7 / (D * D), None, ALU.mult, inc=True, selfsync=True)
    ctx.wait("tensor", sem["dve"], c_d)
    ctx.wait("tensor", sem["sd"], c_bl)
    mm(ctx, PS[0:128, LB:LB + 1], ones_r[0:1, 0:128], sm["m"][0:1, 0:1], start=True, stop=True)
    c_mm2 = mm(ctx, PS[0:128, 2 * LB:2 * LB + 1], ones_r[0:1, 0:128], alps[:], start=True, stop=True,
               fence=True, fence_ap=PS[0:1, 2 * LB:2 * LB + 256])
    ctx.wait("vector", sem["pe"], c_mm2)
    vop(ctx, "tensor_copy", delt[:], PS[0:128, LB:LB + 1])
    vop(ctx, "tensor_copy", alph[:], PS[0:128, 2 * LB:2 * LB + 1], selfsync=True)
    vop(ctx, "tensor_scalar", ndelt[:], delt[:], -1.0, None, ALU.mult, selfsync=True)
    c_tw = None
    for ec in range(EC):
        vop(ctx, "tensor_scalar", gtile[:], wl[:, ec * D:(ec + 1) * D], delt[:, 0:1], None, ALU.is_gt)
        vop(ctx, "tensor_scalar", ltile[:], wl[:, ec * D:(ec + 1) * D], ndelt[:, 0:1], None, ALU.is_lt)
        c_tw = vop(ctx, "tensor_tensor", w3[:, ec * D:(ec + 1) * D], gtile[:], ltile[:],
                   ALU.subtract, inc=(ec == EC - 1))
    ctx.wait("tensor", sem["dve"], c_tw)

    evac = {}
    c_xr = None
    for ec in range(EC):
        c_xr = dma(ctx, "sync", yT[:, ec * LB:(ec + 1) * LB],
                   x3T[0, ec * 128:(ec + 1) * 128, 0:LB], "sd")
    evac["c_xr"] = c_xr
    it = 0
    for b in range(B):
        for lb in range(NLB):
            xp = it % 2
            xbase = xp * EC * LB
            ctx.wait("sync", sem["pe"], evac.get(("x3b", xp)))
            c_x = None
            for ec in range(EC):
                c_x = dma(ctx, "sync", x3b[:, xbase + ec * LB:xbase + (ec + 1) * LB],
                          x3bf[b, ec * 128:(ec + 1) * 128, lb * LB:(lb + 1) * LB], "sd")
            ctx.wait("tensor", sem["sd"], c_x)
            cm = None
            for ot in range(EC):
                wb = ot % 2
                ctx.wait("tensor", sem["dve"], evac.get(wb))
                for ec in range(EC):
                    cm = mm(ctx, PS[:, wb * LB:(wb + 1) * LB],
                            w3[:, ec * D + ot * 128:ec * D + ot * 128 + 128],
                            x3b[:, xbase + ec * LB:xbase + (ec + 1) * LB],
                            start=(ec == 0), stop=(ec == EC - 1), fence=(ec == EC - 1))
                ctx.wait("vector", sem["pe"], cm)
                if ot == 0:
                    ctx.wait("vector", sem["sd"], evac.get("c_xr"))
                vop(ctx, "tensor_scalar", tmpc[:], PS[:, wb * LB:(wb + 1) * LB],
                    alph[:, 0:1], bl[:, ot:ot + 1], ALU.mult, ALU.add)
                evac[wb] = vop(ctx, "tensor_tensor", yT[:, ot * LB:(ot + 1) * LB],
                               yT[:, ot * LB:(ot + 1) * LB], tmpc[:], ALU.add, inc=True)
            evac[("x3b", xp)] = cm
            ln_apply(ctx, PS, evac, yT, sm, ones_cf, ones_r, cst[0:1, 0:1],
                     sum_banks=(2, 3), bc_banks=(4, 5))
            c_st = None
            for ec in range(EC):
                ctx.wait("sync", sem["dve"], evac[("ln", ec)])
                c_st = dma(ctx, "sync", outT[b, ec * 128:(ec + 1) * 128, lb * LB:(lb + 1) * LB],
                           yT[:, ec * LB:(ec + 1) * LB], "sst")
            ctx.wait("sync", sem["sst"], c_st)
            it += 1
            if it < B * NLB:
                nb2, nlb = divmod(it, NLB)
                c_xr = None
                for ec in range(EC):
                    c_xr = dma(ctx, "sync", yT[:, ec * LB:(ec + 1) * LB],
                               x3T[nb2, ec * 128:(ec + 1) * 128, nlb * LB:(nlb + 1) * LB], "sd")
                evac["c_xr"] = c_xr



def build_program(stage_names):
    nc = bass.Bass("TRN2", target_bir_lowering=False)
    ctx = Ctx(nc)
    last = stage_names[-1]
    d = {}
    kinds = {}

    def dram(name, shape, dtype, kind):
        d[name] = nc.dram_tensor(name, shape, dtype, kind=kind).ap()
        kinds[name] = kind
        return d[name]

    def io(name, produced_by, consumed_by):
        if produced_by in stage_names and consumed_by in stage_names:
            kind = "ExternalOutput" if DEBUG_IO else "Internal"
        elif produced_by in stage_names:
            kind = "ExternalOutput"
        else:
            kind = "ExternalInput"
        return dram(name, [B, D, S], F32, kind)

    def iobf(name, produced_by, consumed_by, shape):
        if produced_by in stage_names:
            kind = "ExternalOutput"
        else:
            kind = "ExternalInput"
        return dram(name, shape, BF16, kind)

    if "A" in stage_names:
        dram("xT", [B, D, S], F32, "ExternalInput")
        for n in ("wq2", "wk2", "wv2", "wo2"):
            dram(n, [D, D], F32, "ExternalInput")
    if "A" in stage_names or "B" in stage_names:
        io("x1T", "A", "B")
        iobf("x1bf", "A", "B", [B, D, S])
    if "B" in stage_names:
        dram("wgT", [D, FF], I32, "ExternalInput")
        dram("wuT", [D, FF], I32, "ExternalInput")
        dram("wdH", [EC, 128, FC * 128], I32, "ExternalInput")
        dram("wdbf", [EC, 128, FC * 128], BF16, "ExternalOutput")
    if "B" in stage_names or "C" in stage_names:
        io("x2T", "B", "C")
    if "C" in stage_names:
        dram("owT", [D, D], F32, "ExternalInput")
        dram("gw7", [128, 56], F32, "ExternalInput")
        dram("cw7", [128, 56], F32, "ExternalInput")
        for n in ("gb8", "cb8", "ld8", "ob8", "nw8"):
            dram(n, [128, 8], F32, "ExternalInput")
    if "C" in stage_names or "D" in stage_names:
        io("x3T", "C", "D")
        iobf("x3bf", "C", "D", [B, D, S])
    if "D" in stage_names:
        dram("wlT", [D, D], F32, "ExternalInput")
        dram("alphah", [1, 1], F32, "ExternalInput")
        dram("blh", [128, 8], F32, "ExternalInput")
        dram("outT", [B, D, S], F32, "ExternalOutput")

    with ExitStack() as outer:
        ctx.make_sems(outer)
        PS = outer.enter_context(nc.psum_tensor("PS", [128, 4096], F32))
        ctx.zf = outer.enter_context(nc.sbuf_tensor("zfence", [1, 256], BF16))
        c_zf = vop(ctx, "memset", ctx.zf[:], 0.0, inc=True)
        ctx.wait("tensor", ctx.sems["dve"], c_zf)
        block = outer.enter_context(nc.Block())
        for sname in stage_names:
            with ExitStack() as st_:
                if sname == "A":
                    stage_A(ctx, st_, nc, PS, d["xT"], d["wq2"], d["wk2"], d["wv2"],
                            d["wo2"], d["x1T"], d["x1bf"])
                elif sname == "B":
                    stage_B(ctx, st_, nc, PS, d["x1T"], d["x1bf"], d["wgT"], d["wuT"],
                            d["wdH"], d["wdbf"], d["x2T"])
                elif sname == "C":
                    cparams = (d["gw7"], d["cw7"], d["gb8"], d["cb8"], d["ld8"],
                               d["ob8"], d["nw8"])
                    stage_C(ctx, st_, nc, PS, d["x2T"], d["owT"], cparams, d["x3T"],
                            d["x3bf"])
                elif sname == "D":
                    stage_D(ctx, st_, nc, PS, d["x3T"], d["x3bf"], d["wlT"], d["alphah"],
                            d["blh"], d["outT"])
                else:
                    raise ValueError(sname)
                if sname != last:
                    ctx.barrier()
                else:
                    ctx.wait("sync", ctx.sems["sst"], ctx.sems["sst"].n)
                    ctx.wait("sync", ctx.sems["sd"], ctx.sems["sd"].n)
                    ctx.wait("gpsimd", ctx.sems["sg"], ctx.sems["sg"].n)
                ctx.replay(block)
    return nc, d, kinds



def prep_attn_weights(W_q, W_k, W_v, W_o):
    wq2 = np.ascontiguousarray(np.transpose(np.asarray(W_q), (1, 0, 2)).reshape(D, D))
    wk2 = np.ascontiguousarray(np.transpose(np.asarray(W_k), (1, 0, 2)).reshape(D, D))
    wv2 = np.ascontiguousarray(np.transpose(np.asarray(W_v), (1, 0, 2)).reshape(D, D))
    wo2 = np.ascontiguousarray(np.asarray(W_o))
    return wq2, wk2, wv2, wo2


def xT_of(x):
    return np.ascontiguousarray(np.transpose(np.asarray(x), (0, 2, 1)))


def vec8(v):
    return np.ascontiguousarray(np.asarray(v).reshape(8, 128).T)


def prep_all(inputs):
    wq2, wk2, wv2, wo2 = prep_attn_weights(inputs["L0_attn_W_q"], inputs["L0_attn_W_k"],
                                           inputs["L0_attn_W_v"], inputs["L0_attn_W_o"])
    m = {
        "wq2": wq2, "wk2": wk2, "wv2": wv2, "wo2": wo2,
        "wgT": np.ascontiguousarray(np.asarray(inputs["L0_ffn_w_gate"]).T),
        "wuT": np.ascontiguousarray(np.asarray(inputs["L0_ffn_w_up"]).T),
        "wdH": np.ascontiguousarray(
            np.asarray(inputs["L0_ffn_w_down"]).T.reshape(FC, 128, EC, 128)
            .transpose(2, 1, 0, 3).reshape(EC, 128, FC * 128)),
        "owT": np.ascontiguousarray(np.asarray(inputs["L1_attn_out_w"]).T),
        "gw7": np.ascontiguousarray(
            np.asarray(inputs["L1_attn_gate_w"])[:, 0, :].reshape(8, 128, 7)
            .transpose(1, 0, 2).reshape(128, 56)),
        "cw7": np.ascontiguousarray(
            np.asarray(inputs["L1_attn_cand_w"])[:, 0, :].reshape(8, 128, 7)
            .transpose(1, 0, 2).reshape(128, 56)),
        "gb8": vec8(inputs["L1_attn_gate_b"]),
        "cb8": vec8(inputs["L1_attn_cand_b"]),
        "ld8": vec8(inputs["L1_attn_log_decay"]),
        "ob8": vec8(inputs["L1_attn_out_b"]),
        "nw8": vec8(inputs["L1_attn_norm_w"]),
        "wlT": np.ascontiguousarray(np.asarray(inputs["L1_ffn_weight"]).T),
        "alphah": np.asarray(inputs["L1_ffn_alpha"], dtype=np.float32).reshape(1, 1),
        "blh": vec8(inputs["L1_ffn_bias"]),
    }
    return m


STAGE_GROUPS = [["A", "B", "C", "D"]]


def kernel(**inputs):
    shared = prep_all(inputs)
    x = np.asarray(inputs["L0_attn_x"])
    xTs = [xT_of(x[c * B:(c + 1) * B]) for c in range(NCORES)]

    per_core_state = [dict() for _ in range(NCORES)]
    for c in range(NCORES):
        per_core_state[c]["xT"] = xTs[c]

    for gi, group in enumerate(STAGE_GROUPS):
        nc, d, kinds = build_program(group)
        in_maps = []
        for c in range(NCORES):
            im = {k: (shared[k] if k in shared else per_core_state[c][k])
                  for k, kind in kinds.items() if kind == "ExternalInput"}
            in_maps.append(im)
        res = run_bass_kernel_spmd(nc, in_maps, core_ids=list(range(NCORES)))
        for c in range(NCORES):
            for k, v in res.results[c].items():
                per_core_state[c][k] = v
    outs = [np.transpose(per_core_state[c]["outT"], (0, 2, 1)) for c in range(NCORES)]
    return np.ascontiguousarray(np.concatenate(outs, axis=0), dtype=np.float32)


# version 08 (final)
# speedup vs baseline: 1.0042x; 1.0042x over previous
import numpy as np
import ml_dtypes
from contextlib import ExitStack

import concourse.bass as bass
import concourse.mybir as mybir
from concourse.bass_utils import run_bass_kernel_spmd

F32 = mybir.dt.float32
BF16 = mybir.dt.bfloat16
I32 = mybir.dt.int32
AF = mybir.ActivationFunctionType
ALU = mybir.AluOpType
AX = mybir.AxisListType

NCORES = 8
B = 4
S = 2048
D = 1024
EC = 8
NP = 8
STN = 16
LB = 512
NLB = S // LB
FF = 3072
FC = 24
LN_EPS = 1e-5
RMS_EPS = 1.1920929e-07
NEG_LN8 = -2.0794415416798357
DEBUG_C = False
DEBUG_IO = True
QSCALE = 0.125


class Sem:
    def __init__(self, h):
        self.h = h
        self.n = 0

    def bump(self, k=1):
        self.n += k
        return self.n


class Ctx:
    ENGINES = ("sync", "tensor", "vector", "scalar", "gpsimd")

    def __init__(self, nc):
        self.nc = nc
        self.stack = None
        self.sems = {}
        self.streams = {e: [] for e in self.ENGINES}
        self.nbar = 0

    def make_sems(self, stack):
        for name in ("pe", "act", "dve", "pool", "sd", "sg", "sst", "bar"):
            self.sems[name] = Sem(stack.enter_context(self.nc.semaphore(name)))

    def emit(self, eng, fn):
        self.streams[eng].append(fn)

    def wait(self, eng, sem, val):
        if val is None or val <= 0:
            return
        self.emit(eng, lambda E, s=sem.h, v=val: E.wait_ge(s, v))

    def barrier(self):
        bar = self.sems["bar"]
        self.wait("sync", self.sems["sd"], self.sems["sd"].n)
        self.wait("sync", self.sems["sst"], self.sems["sst"].n)
        self.wait("gpsimd", self.sems["sg"], self.sems["sg"].n)
        for e in self.ENGINES:
            self.emit(e, lambda E, s=bar.h: E.sem_inc(s, 1))
        self.nbar += 1
        tgt = 5 * self.nbar
        bar.n = tgt
        for e in self.ENGINES:
            self.emit(e, lambda E, s=bar.h, v=tgt: E.wait_ge(s, v))

    def replay(self, block):
        for eng in self.ENGINES:
            ops = self.streams[eng]
            if not ops:
                continue
            sec = getattr(block, eng)

            @sec
            def _(E, ops=ops):
                for fn in ops:
                    fn(E)
        self.streams = {e: [] for e in self.ENGINES}


def _inc(instr, sem, k=1):
    instr.then_inc(sem.h, k)


def mm(ctx, out, lhsT, rhs, start, stop, tp=None, inc=False, fence=False, fence_ap=None):
    sem = ctx.sems["pe"]
    if fence:
        ctx.emit("tensor", lambda E, o=out, l=lhsT, r=rhs, st=start, sp=stop, tp=tp:
                 E.matmul(o, l, r, start=st, stop=sp, tile_position=tp))
        n = sem.bump()
        zf = ctx.zf
        if fence_ap is not None:
            fo = fence_ap
        else:
            w = min(256, int(out.shape[1]))
            fo = out[0:1, 0:w]
        fw = int(fo.shape[1])
        ctx.emit("tensor", lambda E, o=fo, z=zf, fw=fw:
                 _inc(E.matmul(o, z[0:1, 0:1], z[0:1, 0:fw], start=False, stop=True), sem))
        return n
    if inc:
        n = sem.bump()
        ctx.emit("tensor", lambda E, o=out, l=lhsT, r=rhs, st=start, sp=stop, tp=tp:
                 _inc(E.matmul(o, l, r, start=st, stop=sp, tile_position=tp), sem))
        return n
    ctx.emit("tensor", lambda E, o=out, l=lhsT, r=rhs, st=start, sp=stop, tp=tp:
             E.matmul(o, l, r, start=st, stop=sp, tile_position=tp))
    return None


def act(ctx, out, in_, func, bias=None, scale=None, inc=False, selfsync=False):
    sem = ctx.sems["act"]
    kw = {}
    if bias is not None:
        kw["bias"] = bias
    if scale is not None:
        kw["scale"] = scale
    if inc or selfsync:
        n = sem.bump()
        ctx.emit("scalar", lambda E, o=out, i=in_, f=func, kw=kw: _inc(E.activation(o, i, f, **kw), sem))
        if selfsync:
            ctx.wait("scalar", sem, n)
        return n
    ctx.emit("scalar", lambda E, o=out, i=in_, f=func, kw=kw: E.activation(o, i, f, **kw))
    return None


def vop(ctx, name, *args, inc=False, selfsync=False, eng="vector", **kw):
    sem = ctx.sems["dve" if eng == "vector" else "pool"]
    if inc or selfsync:
        n = sem.bump()
        ctx.emit(eng, lambda E, a=args, kw=kw: _inc(getattr(E, name)(*a, **kw), sem))
        if selfsync:
            ctx.wait(eng if eng != "vector" else "vector", sem, n)
        return n
    ctx.emit(eng, lambda E, a=args, kw=kw: getattr(E, name)(*a, **kw))
    return None


def dma(ctx, queue, out, in_, sem_name):
    sem = ctx.sems[sem_name]
    n = sem.bump(16)
    ctx.emit(queue, lambda E, o=out, i=in_: _inc(E.dma_start(out=o, in_=i), sem, 16))
    return n



def ln_apply(ctx, PS, evac, yT, sm, ones_cf, ones_r, eps, rms=False, nw=None,
             sum_banks=(2, 3), bc_banks=(4, 5)):
    sem = ctx.sems
    c_ts = None
    if not rms:
        vop(ctx, "tensor_copy", sm["ts"][:], yT[:, 0:LB])
        for ec in range(1, EC):
            c_ts = vop(ctx, "tensor_tensor", sm["ts"][:], sm["ts"][:], yT[:, ec * LB:(ec + 1) * LB],
                       ALU.add, inc=(ec == EC - 1))
    c_tsq = None
    for ec in range(EC):
        vop(ctx, "tensor_tensor", sm["sqc"][:], yT[:, ec * LB:(ec + 1) * LB], yT[:, ec * LB:(ec + 1) * LB], ALU.mult)
        if ec == 0:
            vop(ctx, "tensor_copy", sm["tsq"][:], sm["sqc"][:])
        else:
            c_tsq = vop(ctx, "tensor_tensor", sm["tsq"][:], sm["tsq"][:], sm["sqc"][:],
                        ALU.add, inc=(ec == EC - 1))
    b2, b3 = sum_banks
    for bk in sum_banks:
        ctx.wait("tensor", sem["dve"], evac.get(bk))
        ctx.wait("tensor", sem["act"], evac.get(("act", bk)))
    ctx.wait("tensor", sem["dve"], c_tsq)
    if not rms:
        mm(ctx, PS[0:1, b2 * LB:(b2 + 1) * LB], ones_cf[:, 0:1], sm["ts"][:], start=True, stop=True)
    c_mm = mm(ctx, PS[0:1, b3 * LB:(b3 + 1) * LB], ones_cf[:, 0:1], sm["tsq"][:], start=True, stop=True, fence=True)
    ctx.wait("vector", sem["pe"], c_mm)
    if not rms:
        vop(ctx, "tensor_scalar", sm["m"][:], PS[0:1, b2 * LB:(b2 + 1) * LB], 1.0 / D, None, ALU.mult)
        vop(ctx, "tensor_scalar", sm["var"][:], PS[0:1, b3 * LB:(b3 + 1) * LB], 1.0 / D, None,
            ALU.mult, selfsync=True)
        vop(ctx, "tensor_tensor", sm["sqc"][0:1, 0:LB], sm["m"][:], sm["m"][:], ALU.mult,
            selfsync=True)
        ev23 = vop(ctx, "tensor_tensor", sm["var"][:], sm["var"][:], sm["sqc"][0:1, 0:LB],
                   ALU.subtract, inc=True, selfsync=True)
    else:
        ev23 = vop(ctx, "tensor_scalar", sm["var"][:], PS[0:1, b3 * LB:(b3 + 1) * LB], 1.0 / D, None,
                   ALU.mult, inc=True, selfsync=True)
    evac[b2] = evac[b3] = ev23
    ctx.wait("scalar", sem["dve"], ev23)
    act(ctx, sm["r"][:], sm["var"][:], AF.Ln, bias=eps, selfsync=True)
    c_r = act(ctx, sm["r"][:], sm["r"][:], AF.Exp, scale=-0.5, inc=True, selfsync=True)
    c_mr = None
    if not rms:
        ctx.wait("vector", sem["act"], c_r)
        c_mr = vop(ctx, "tensor_tensor", sm["mr"][:], sm["m"][:], sm["r"][:], ALU.mult, inc=True)
    b4, b5 = bc_banks
    for bk in bc_banks:
        ctx.wait("tensor", sem["dve"], evac.get(bk))
        ctx.wait("tensor", sem["act"], evac.get(("act", bk)))
    ctx.wait("tensor", sem["act"], c_r)
    c_b = mm(ctx, PS[:, b4 * LB:(b4 + 1) * LB], ones_r[0:1, 0:128], sm["r"][:], start=True, stop=True,
             fence=rms)
    if not rms:
        ctx.wait("tensor", sem["dve"], c_mr)
        c_b = mm(ctx, PS[:, b5 * LB:(b5 + 1) * LB], ones_r[0:1, 0:128], sm["mr"][:], start=True, stop=True, fence=True)
    ctx.wait("vector", sem["pe"], c_b)
    last = None
    for ec in range(EC):
        sl = yT[:, ec * LB:(ec + 1) * LB]
        vop(ctx, "tensor_tensor", sl, sl, PS[:, b4 * LB:(b4 + 1) * LB], ALU.mult)
        if not rms:
            last = vop(ctx, "tensor_tensor", sl, sl, PS[:, b5 * LB:(b5 + 1) * LB], ALU.subtract, inc=True)
        else:
            last = vop(ctx, "tensor_scalar", sl, sl, nw[:, ec:ec + 1], None, ALU.mult, inc=True)
        evac[("ln", ec)] = last
    evac[b4] = evac[b5] = last
    return last


def stage_A(ctx, st_, nc, PS, xT, wq2, wk2, wv2, wo2, x1T, x1bf):
    E = st_.enter_context
    sem = ctx.sems

    u0 = E(nc.sbuf_tensor("a_u0", [128, 16384], BF16))
    wbuf = E(nc.sbuf_tensor("a_wbuf", [128, EC * D], BF16))
    wo = E(nc.sbuf_tensor("a_wo", [128, EC * D], BF16))
    qT = E(nc.sbuf_tensor("a_qT", [128, NP * S], BF16))
    kT = E(nc.sbuf_tensor("a_kT", [128, NP * S], BF16))
    vv = E(nc.sbuf_tensor("a_vv", [128, STN * D], BF16))
    yT = E(nc.sbuf_tensor("a_yT", [128, EC * LB], F32))
    xstg = yT
    stgb = E(nc.sbuf_tensor("a_stgb", [128, 2 * LB], BF16))
    accA = E(nc.sbuf_tensor("a_accA", [128, LB], F32))[:]
    accB = E(nc.sbuf_tensor("a_accB", [128, LB], F32))[:]
    bcS = E(nc.sbuf_tensor("a_bcS", [128, LB], F32))[:]
    rr = E(nc.sbuf_tensor("a_rr", [1, 2 * LB], F32))
    ones_c = E(nc.sbuf_tensor("a_ones_c", [128, 1], BF16))
    ones_cf = E(nc.sbuf_tensor("a_ones_cf", [128, 1], F32))
    ones_r = E(nc.sbuf_tensor("a_ones_r", [1, 128], F32))
    qkm = E(nc.sbuf_tensor("a_qkm", [1, 4], F32))
    negB = E(nc.sbuf_tensor("a_negB", [128, 1], F32))
    tmp1 = E(nc.sbuf_tensor("a_tmp1", [1, 1], F32))
    ts_t = E(nc.sbuf_tensor("a_ts", [128, LB], F32))
    tsq_t = E(nc.sbuf_tensor("a_tsq", [128, LB], F32))
    sqc_t = E(nc.sbuf_tensor("a_sqc", [128, LB], F32))
    row4 = E(nc.sbuf_tensor("a_row4", [1, 3 * LB], F32))
    cst = E(nc.sbuf_tensor("a_cst", [1, 2], F32))
    sm = {"ts": ts_t, "tsq": tsq_t, "sqc": sqc_t,
          "m": row4[0:1, 0:LB], "var": row4[0:1, LB:2 * LB],
          "r": row4[0:1, 2 * LB:3 * LB], "mr": row4[0:1, 0:LB]}

    xbT = u0
    eT = u0[:, 0:8 * 1024]
    accb = u0[:, 8192:8192 + 2 * LB]
    concatT = u0[:, 9216:9216 + EC * LB]
    sqt = u0[:, 0:S]

    c_ones = None
    vop(ctx, "memset", ones_c[:], 1.0)
    vop(ctx, "memset", ones_cf[:], 1.0)
    vop(ctx, "memset", cst[0:1, 0:1], LN_EPS)
    vop(ctx, "memset", cst[0:1, 1:2], NEG_LN8)
    c_ones = vop(ctx, "memset", ones_r[:], 1.0, inc=True)
    ctx.wait("scalar", ctx.sems["dve"], c_ones)

    def bank(i, rows=128, w=LB):
        return PS[0:rows, i * LB:i * LB + w]

    evac = {}
    c_free_u0 = None
    c_free_w = None
    stg_dve = {0: None, 1: None}

    def ldcast(dst_slice, src_ap, half, width):
        ctx.wait("sync", sem["dve"], stg_dve[half])
        c_l = dma(ctx, "sync", xstg[:, half * S:half * S + width], src_ap, "sd")
        ctx.wait("vector", sem["sd"], c_l)
        c_c = vop(ctx, "tensor_copy", dst_slice, xstg[:, half * S:half * S + width], inc=True)
        stg_dve[half] = c_c
        return c_c

    for b in range(B):
        ctx.wait("vector", sem["pe"], c_free_u0)
        c_xb = None
        for ec in range(EC):
            c_xb = ldcast(xbT[:, ec * S:(ec + 1) * S], xT[b, ec * 128:(ec + 1) * 128, :], ec % 2, S)
        ctx.wait("vector", sem["pe"], c_free_w)
        c_wq = None
        for ec in range(EC):
            c_wq = ldcast(wbuf[:, ec * D:(ec + 1) * D], wq2[ec * 128:(ec + 1) * 128, :], ec % 2, D)
        if b == 0:
            c_wo = None
            for ec in range(EC):
                c_wo = ldcast(wo[:, ec * D:(ec + 1) * D], wo2[ec * 128:(ec + 1) * 128, :], ec % 2, D)

        ctx.wait("tensor", sem["dve"], c_ones)

        def proj(dst, c_w):
            ctx.wait("tensor", sem["dve"], max(c_xb, c_w))
            last = None
            for p in range(NP):
                for nb in range(NLB):
                    bk = nb
                    ctx.wait("tensor", sem["dve"], evac.get(bk))
                    ctx.wait("tensor", sem["act"], evac.get(("act", bk)))
                    cm = None
                    for ec in range(EC):
                        cm = mm(ctx, bank(bk), wbuf[:, ec * D + p * 128:ec * D + p * 128 + 128],
                                xbT[:, ec * S + nb * LB:ec * S + nb * LB + LB],
                                start=(ec == 0), stop=(ec == EC - 1), fence=(ec == EC - 1))
                    ctx.wait("vector", sem["pe"], cm)
                    evac[bk] = vop(ctx, "tensor_copy", dst[:, p * S + nb * LB:p * S + nb * LB + LB],
                                   bank(bk), inc=True)
            return cm

        c_pe_q = proj(qT, c_wq)
        ctx.wait("vector", sem["pe"], c_pe_q)
        c_wk = None
        for ec in range(EC):
            c_wk = ldcast(wbuf[:, ec * D:(ec + 1) * D], wk2[ec * 128:(ec + 1) * 128, :], ec % 2, D)
        c_pe_k = proj(kT, c_wk)
        ctx.wait("vector", sem["pe"], c_pe_k)
        c_wv = None
        for ec in range(EC):
            c_wv = ldcast(wbuf[:, ec * D:(ec + 1) * D], wv2[ec * 128:(ec + 1) * 128, :], ec % 2, D)

        evac["yT_done"] = c_wv
        ctx.wait("tensor", sem["dve"], c_wv)
        c_pe_v = None
        for st in range(STN):
            for half in range(2):
                ctx.wait("tensor", sem["dve"], evac.get(4 + half))
            cm = None
            for ec in range(EC):
                for half in range(2):
                    cm = mm(ctx, bank(4 + half), xbT[:, ec * S + st * 128:ec * S + st * 128 + 128],
                            wbuf[:, ec * D + half * LB:ec * D + half * LB + LB],
                            start=(ec == 0), stop=(ec == EC - 1), fence=(ec == EC - 1 and half == 1))
            ctx.wait("vector", sem["pe"], cm)
            evac[4] = evac[5] = vop(ctx, "tensor_copy", vv[:, st * D:st * D + D],
                                    PS[:, 4 * LB:4 * LB + D], inc=True)
            c_pe_v = cm

        c_qk = vop(ctx, "memset", qkm[:], 0.0, inc=True)
        for j, src in enumerate((qT, kT)):
            for p in range(NP):
                ctx.wait("vector", sem["pe"], c_pe_v)
                csq = vop(ctx, "tensor_tensor", sqt, src[:, p * S:(p + 1) * S],
                          src[:, p * S:(p + 1) * S], ALU.mult, inc=True)
                for nb in range(NLB):
                    bk = nb % 2
                    ctx.wait("tensor", sem["dve"], evac.get(bk))
                    ctx.wait("tensor", sem["dve"], csq)
                    cm = mm(ctx, bank(bk, rows=1), ones_c[:, 0:1], sqt[:, nb * LB:(nb + 1) * LB],
                            start=True, stop=True, fence=True)
                    ctx.wait("vector", sem["pe"], cm)
                    vop(ctx, "tensor_reduce", tmp1[:], bank(bk, rows=1), AX.X, ALU.max,
                        selfsync=True)
                    c_qk = vop(ctx, "tensor_tensor", qkm[0:1, j:j + 1], qkm[0:1, j:j + 1],
                               tmp1[:], ALU.max, inc=True, selfsync=True)
                    evac[bk] = c_qk
        ctx.wait("scalar", sem["dve"], c_qk)
        act(ctx, qkm[0:1, 2:3], qkm[0:1, 0:1], AF.Ln)
        c_l = act(ctx, qkm[0:1, 3:4], qkm[0:1, 1:2], AF.Ln, inc=True)
        ctx.wait("vector", sem["act"], c_l)
        c_s = vop(ctx, "tensor_tensor", tmp1[:], qkm[0:1, 2:3], qkm[0:1, 3:4], ALU.add,
                  inc=True, selfsync=True)
        ctx.wait("scalar", sem["dve"], c_s)
        c_b = act(ctx, tmp1[:], tmp1[:], AF.Exp, scale=0.5, bias=cst[0:1, 1:2], inc=True)
        ctx.wait("vector", sem["act"], c_b)
        c_nb = vop(ctx, "tensor_scalar", tmp1[:], tmp1[:], -1.0, None, ALU.mult,
                   inc=True, selfsync=True)
        ctx.wait("tensor", sem["dve"], c_nb)
        bk = 2
        ctx.wait("tensor", sem["dve"], evac.get(bk))
        cm = mm(ctx, bank(bk, w=1), ones_r[0:1, 0:128], tmp1[:], start=True, stop=True, fence=True,
                fence_ap=bank(bk, rows=1, w=256))
        ctx.wait("vector", sem["pe"], cm)
        evac[bk] = c_negB = vop(ctx, "tensor_copy", negB[:], bank(bk, w=1), inc=True)
        ctx.wait("scalar", sem["dve"], c_negB)

        for lb in range(NLB):
            c_xr = None
            ctx.wait("sync", sem["dve"], evac.get("yT_done"))
            for ec in range(EC):
                c_xr = dma(ctx, "sync", yT[:, ec * LB:(ec + 1) * LB],
                           xT[b, ec * 128:(ec + 1) * 128, lb * LB:(lb + 1) * LB], "sd")
            for p in range(NP):
                cb = 4 + (p % 2)
                ctx.wait("tensor", sem["dve"], evac.get(cb))
                cav = None
                cp = None
                for rnd in range(2):
                    exp_done = []
                    for s8 in range(8):
                        st = rnd * 8 + s8
                        sb = (s8 % 2) * 2
                        for bkx in (sb, sb + 1):
                            ctx.wait("tensor", sem["dve"], evac.get(bkx))
                            ctx.wait("tensor", sem["act"], evac.get(("act", bkx)))
                            ctx.wait("tensor", sem["pool"], evac.get(("pool", bkx)))
                        mm(ctx, PS[:, sb * LB:sb * LB + LB],
                           kT[0:64, p * S + st * 128:p * S + st * 128 + 128],
                           qT[0:64, p * S + lb * LB:p * S + lb * LB + LB],
                           start=True, stop=True, tp=(0, 0))
                        cm = mm(ctx, PS[:, sb * LB + LB:sb * LB + 2 * LB],
                                kT[64:128, p * S + st * 128:p * S + st * 128 + 128],
                                qT[64:128, p * S + lb * LB:p * S + lb * LB + LB],
                                start=True, stop=True, tp=(64, 0), fence=True)
                        evac[sb] = evac[sb + 1] = None
                        ctx.wait("scalar", sem["pe"], cm)
                        ce = act(ctx, eT[:, s8 * 1024:(s8 + 1) * 1024], PS[:, sb * LB:sb * LB + 2 * LB],
                                 AF.Exp, bias=negB[:, 0:1], scale=QSCALE, inc=True)
                        evac[("act", sb)] = evac[("act", sb + 1)] = ce
                        exp_done.append(ce)
                        ctx.wait("gpsimd", sem["act"], ce)
                        if st == 0:
                            vop(ctx, "tensor_copy", accA, eT[:, 0:LB], eng="gpsimd")
                            cp = vop(ctx, "tensor_copy", accB, eT[:, LB:2 * LB], eng="gpsimd", inc=True)
                        else:
                            vop(ctx, "tensor_tensor", accA, accA,
                                eT[:, s8 * 1024:s8 * 1024 + LB], ALU.add, eng="gpsimd")
                            cp = vop(ctx, "tensor_tensor", accB, accB,
                                     eT[:, s8 * 1024 + LB:s8 * 1024 + 2 * LB], ALU.add, eng="gpsimd", inc=True)
                    for s8 in range(8):
                        st = rnd * 8 + s8
                        ctx.wait("tensor", sem["act"], exp_done[s8])
                        mm(ctx, PS[0:64, cb * LB:(cb + 1) * LB],
                           vv[:, st * D + (2 * p) * 64:st * D + (2 * p) * 64 + 64],
                           eT[:, s8 * 1024:s8 * 1024 + LB],
                           start=(st == 0), stop=(st == STN - 1), tp=(0, 0))
                        cav = mm(ctx, PS[64:128, cb * LB:(cb + 1) * LB],
                                 vv[:, st * D + (2 * p + 1) * 64:st * D + (2 * p + 1) * 64 + 64],
                                 eT[:, s8 * 1024 + LB:s8 * 1024 + 2 * LB],
                                 start=(st == 0), stop=(st == STN - 1), tp=(0, 64),
                                 fence=(st == STN - 1))
                vop(ctx, "tensor_copy", accb[0:128, 0:LB], accA, eng="gpsimd")
                c_accb = vop(ctx, "tensor_copy", accb[0:128, LB:2 * LB], accB, eng="gpsimd", inc=True)
                ctx.wait("tensor", sem["pool"], c_accb)
                for bkx in (6, 7):
                    ctx.wait("tensor", sem["dve"], evac.get(bkx))
                mm(ctx, PS[0:1, 6 * LB:7 * LB], ones_c[:, 0:1], accb[0:128, 0:LB], start=True, stop=True)
                cdn = mm(ctx, PS[0:1, 7 * LB:8 * LB], ones_c[:, 0:1], accb[0:128, LB:2 * LB],
                         start=True, stop=True, fence=True)
                ctx.wait("vector", sem["pe"], cdn)
                vop(ctx, "reciprocal", rr[0:1, 0:LB], PS[0:1, 6 * LB:7 * LB])
                c_rr = vop(ctx, "reciprocal", rr[0:1, LB:2 * LB], PS[0:1, 7 * LB:8 * LB], inc=True)
                evac[6] = evac[7] = c_rr
                bb = 0
                ctx.wait("tensor", sem["act"], evac.get(("act", bb)))
                ctx.wait("tensor", sem["pool"], evac.get(("pool", bb)))
                ctx.wait("tensor", sem["dve"], c_rr)
                mm(ctx, PS[0:64, bb * LB:bb * LB + LB], ones_r[0:1, 0:64], rr[0:1, 0:LB],
                   start=True, stop=True, tp=(0, 0))
                cbc = mm(ctx, PS[64:128, bb * LB:bb * LB + LB], ones_r[0:1, 0:64], rr[0:1, LB:2 * LB],
                         start=True, stop=True, tp=(0, 64), fence=True)
                ctx.wait("vector", sem["pe"], cbc)
                cbs = vop(ctx, "tensor_copy", bcS, PS[:, bb * LB:bb * LB + LB], inc=True)
                evac[bb] = cbs
                evac[("act", bb)] = None
                ctx.wait("vector", sem["pe"], cav)
                evac[cb] = vop(ctx, "tensor_tensor", concatT[:, p * LB:(p + 1) * LB],
                               PS[:, cb * LB:(cb + 1) * LB], bcS, ALU.mult, inc=True)
            c_concat = evac[4 + ((NP - 1) % 2)]

            ctx.wait("tensor", sem["dve"], c_concat)
            for ot in range(EC):
                wb = ot % 2
                ctx.wait("tensor", sem["dve"], evac.get(wb))
                ctx.wait("tensor", sem["act"], evac.get(("act", wb)))
                cm = None
                for ec in range(EC):
                    cm = mm(ctx, bank(wb), wo[:, ec * D + ot * 128:ec * D + ot * 128 + 128],
                            concatT[:, ec * LB:(ec + 1) * LB], start=(ec == 0), stop=(ec == EC - 1),
                            fence=(ec == EC - 1))
                ctx.wait("vector", sem["pe"], cm)
                if ot == 0:
                    ctx.wait("vector", sem["sd"], c_xr)
                evac[wb] = vop(ctx, "tensor_tensor", yT[:, ot * LB:(ot + 1) * LB], bank(wb),
                               yT[:, ot * LB:(ot + 1) * LB], ALU.add, inc=True)

            ln_apply(ctx, PS, evac, yT, sm, ones_cf, ones_r, cst[0:1, 0:1])
            c_store = None
            for ec in range(EC):
                hb_ = ec % 2
                ctx.wait("sync", sem["dve"], evac[("ln", ec)])
                c_store = dma(ctx, "sync", x1T[b, ec * 128:(ec + 1) * 128, lb * LB:(lb + 1) * LB],
                              yT[:, ec * LB:(ec + 1) * LB], "sst")
                ctx.wait("vector", sem["sst"], evac.get(("mir", hb_)))
                c_mc = vop(ctx, "tensor_copy", stgb[:, hb_ * LB:(hb_ + 1) * LB],
                           yT[:, ec * LB:(ec + 1) * LB], inc=True)
                ctx.wait("sync", sem["dve"], c_mc)
                evac[("mir", hb_)] = dma(ctx, "sync",
                                         x1bf[b, ec * 128:(ec + 1) * 128, lb * LB:(lb + 1) * LB],
                                         stgb[:, hb_ * LB:(hb_ + 1) * LB], "sst")
            evac["yT_done"] = evac[("ln", EC - 1)]
            ctx.wait("sync", sem["sst"], evac[("mir", (EC - 1) % 2)])

        c_free_u0 = sem["pe"].n
        c_free_w = c_pe_v



def alloc_ln(nc, E, pfx):
    ts_t = E(nc.sbuf_tensor(pfx + "_ts", [128, LB], F32))
    tsq_t = E(nc.sbuf_tensor(pfx + "_tsq", [128, LB], F32))
    sqc_t = E(nc.sbuf_tensor(pfx + "_sqc", [128, LB], F32))
    row4 = E(nc.sbuf_tensor(pfx + "_row4", [1, 3 * LB], F32))
    ones_cf = E(nc.sbuf_tensor(pfx + "_ocf", [128, 1], F32))
    ones_r = E(nc.sbuf_tensor(pfx + "_or", [1, 128], F32))
    cst = E(nc.sbuf_tensor(pfx + "_cst", [1, 2], F32))
    sm = {"ts": ts_t, "tsq": tsq_t, "sqc": sqc_t,
          "m": row4[0:1, 0:LB], "var": row4[0:1, LB:2 * LB],
          "r": row4[0:1, 2 * LB:3 * LB], "mr": row4[0:1, 0:LB]}
    return sm, ones_cf, ones_r, cst


def ln_const_init(ctx, ones_cf, ones_r, cst, eps):
    vop(ctx, "memset", ones_cf[:], 1.0)
    vop(ctx, "memset", cst[0:1, 0:1], eps)
    vop(ctx, "memset", cst[0:1, 1:2], NEG_LN8)
    c = vop(ctx, "memset", ones_r[:], 1.0, inc=True)
    ctx.wait("tensor", ctx.sems["dve"], c)
    ctx.wait("scalar", ctx.sems["dve"], c)
    return c


def stage_B(ctx, st_, nc, PS, x1T, x1bf, wgT, wuT, wdH, wdbf, x2T):
    sem = ctx.sems
    INV127 = 1.0 / 127.0
    INVSQ = 1.0 / (127.0 * 127.0)
    E = st_.enter_context
    wgt = E(nc.sbuf_tensor("b_wgt", [128, EC * FF], BF16))
    wut = E(nc.sbuf_tensor("b_wut", [128, EC * FF], BF16))
    x1b = E(nc.sbuf_tensor("b_x1b", [128, 2 * EC * LB], BF16))
    hbm = E(nc.sbuf_tensor("b_hbm", [128, FC * LB], BF16))
    wdsl = E(nc.sbuf_tensor("b_wdsl", [128, 2 * FC * 128], BF16))
    sgg = E(nc.sbuf_tensor("b_sgg", [128, LB], F32))
    yT = E(nc.sbuf_tensor("b_yT", [128, EC * LB], F32))
    bstg = E(nc.sbuf_tensor("b_bstg", [128, FF], I32))
    sm, ones_cf, ones_r, cst = alloc_ln(nc, E, "b")
    ln_const_init(ctx, ones_cf, ones_r, cst, LN_EPS)
    evac = {}
    c_prev = None
    c_wgu = None
    for j, (dst, src) in enumerate(((wgt, wgT), (wut, wuT))):
        for ec in range(EC):
            ctx.wait("sync", sem["dve"], c_prev)
            c_l = dma(ctx, "sync", bstg[:], src[ec * 128:(ec + 1) * 128, :], "sd")
            ctx.wait("vector", sem["sd"], c_l)
            c_prev = c_wgu = vop(ctx, "tensor_copy", dst[:, ec * FF:(ec + 1) * FF], bstg[:], inc=True)
    c_wdbf = None
    for dt_ in range(EC):
        ctx.wait("sync", sem["dve"], c_prev)
        c_l = dma(ctx, "sync", bstg[:, 0:FC * 128], wdH[dt_], "sd")
        ctx.wait("vector", sem["sd"], c_l)
        ctx.wait("vector", sem["sst"], c_wdbf)
        c_prev = vop(ctx, "tensor_copy", wdsl[:, 0:FC * 128], bstg[:, 0:FC * 128], inc=True)
        ctx.wait("sync", sem["dve"], c_prev)
        c_wdbf = dma(ctx, "sync", wdbf[dt_], wdsl[:, 0:FC * 128], "sst")
    ctx.wait("sync", sem["sst"], c_wdbf)
    c_xr0 = None
    for ec in range(EC):
        c_xr0 = dma(ctx, "sync", yT[:, ec * LB:(ec + 1) * LB],
                    x1T[0, ec * 128:(ec + 1) * 128, 0:LB], "sd")
    evac["c_xr"] = c_xr0
    it = 0
    for b in range(B):
        for lb in range(NLB):
            xp = it % 2
            xbase = xp * EC * LB
            c_x = evac.get("c_x_pref")
            c_wd = evac.get("c_wd_pref")
            if c_x is None:
                ctx.wait("sync", sem["pe"], evac.get(("x1b", xp)))
                for ec in range(EC):
                    c_x = dma(ctx, "sync", x1b[:, xbase + ec * LB:xbase + (ec + 1) * LB],
                              x1bf[b, ec * 128:(ec + 1) * 128, lb * LB:(lb + 1) * LB], "sd")
                ctx.wait("sync", sem["pe"], evac.get(("wdsl", 0)))
                c_wd = dma(ctx, "sync", wdsl[:, 0:FC * 128], wdbf[0], "sd")
            ctx.wait("tensor", sem["sd"], c_x)
            if it == 0:
                ctx.wait("tensor", sem["dve"], c_wgu)
            cu = None
            for ht in range(FC):
                gb = ht % 2
                ub = 2 + ht % 2
                ctx.wait("tensor", sem["act"], evac.get(("act", gb)))
                ctx.wait("tensor", sem["dve"], evac.get(("dve", ub)))
                ctx.wait("tensor", sem["dve"], evac.get(ub))
                cg = None
                for ec in range(EC):
                    cg = mm(ctx, PS[:, gb * LB:(gb + 1) * LB],
                            wgt[:, ec * FF + ht * 128:ec * FF + ht * 128 + 128],
                            x1b[:, xbase + ec * LB:xbase + (ec + 1) * LB],
                            start=(ec == 0), stop=(ec == EC - 1), fence=(ec == EC - 1))
                for ec in range(EC):
                    cu = mm(ctx, PS[:, ub * LB:(ub + 1) * LB],
                            wut[:, ec * FF + ht * 128:ec * FF + ht * 128 + 128],
                            x1b[:, xbase + ec * LB:xbase + (ec + 1) * LB],
                            start=(ec == 0), stop=(ec == EC - 1), fence=(ec == EC - 1))
                ctx.wait("scalar", sem["pe"], cg)
                ctx.wait("scalar", sem["dve"], evac.get("sgg"))
                c_silu = act(ctx, sgg[:], PS[:, gb * LB:(gb + 1) * LB], AF.Silu,
                             scale=INV127, inc=True)
                evac[("act", gb)] = c_silu
                ctx.wait("vector", sem["act"], c_silu)
                ctx.wait("vector", sem["pe"], cu)
                c_h = vop(ctx, "tensor_tensor", hbm[:, ht * LB:(ht + 1) * LB], sgg[:],
                          PS[:, ub * LB:(ub + 1) * LB], ALU.mult, inc=True)
                evac[("dve", ub)] = c_h
                evac["sgg"] = c_h
            evac[("x1b", xp)] = cu
            c_hall = evac["sgg"]
            ctx.wait("tensor", sem["dve"], c_hall)
            cm = None
            for dt_ in range(EC):
                wdp = dt_ % 2
                wdbase = wdp * FC * 128
                ctx.wait("tensor", sem["sd"], c_wd)
                if dt_ < EC - 1:
                    nwdp = (dt_ + 1) % 2
                    ctx.wait("sync", sem["pe"], evac.get(("wdsl", nwdp)))
                    c_wd = dma(ctx, "sync",
                               wdsl[:, nwdp * FC * 128:(nwdp + 1) * FC * 128],
                               wdbf[dt_ + 1], "sd")
                wb = 4 + dt_ % 2
                ctx.wait("tensor", sem["dve"], evac.get(wb))
                for hc in range(FC):
                    cm = mm(ctx, PS[:, wb * LB:(wb + 1) * LB],
                            wdsl[:, wdbase + hc * 128:wdbase + (hc + 1) * 128],
                            hbm[:, hc * LB:(hc + 1) * LB],
                            start=(hc == 0), stop=(hc == FC - 1), fence=(hc == FC - 1))
                evac[("wdsl", wdp)] = cm
                ctx.wait("vector", sem["pe"], cm)
                if dt_ == 0:
                    ctx.wait("vector", sem["sd"], evac.get("c_xr"))
                evac[wb] = vop(ctx, "scalar_tensor_tensor", yT[:, dt_ * LB:(dt_ + 1) * LB],
                               PS[:, wb * LB:(wb + 1) * LB], INVSQ,
                               yT[:, dt_ * LB:(dt_ + 1) * LB], ALU.mult, ALU.add, inc=True)
            evac[2] = evac.get(("dve", 2))
            evac[3] = evac.get(("dve", 3))
            ln_apply(ctx, PS, evac, yT, sm, ones_cf, ones_r, cst[0:1, 0:1],
                     sum_banks=(2, 3), bc_banks=(6, 7))
            c_st = None
            for ec in range(EC):
                ctx.wait("sync", sem["dve"], evac[("ln", ec)])
                c_st = dma(ctx, "sync", x2T[b, ec * 128:(ec + 1) * 128, lb * LB:(lb + 1) * LB],
                           yT[:, ec * LB:(ec + 1) * LB], "sst")
            it += 1
            if it < B * NLB:
                nb2, nlb = divmod(it, NLB)
                nxp = it % 2
                nxbase = nxp * EC * LB
                ctx.wait("sync", sem["pe"], evac.get(("x1b", nxp)))
                c_xp = None
                for ec in range(EC):
                    c_xp = dma(ctx, "sync", x1b[:, nxbase + ec * LB:nxbase + (ec + 1) * LB],
                               x1bf[nb2, ec * 128:(ec + 1) * 128, nlb * LB:(nlb + 1) * LB], "sd")
                evac["c_x_pref"] = c_xp
                ctx.wait("sync", sem["pe"], evac.get(("wdsl", 0)))
                evac["c_wd_pref"] = dma(ctx, "sync", wdsl[:, 0:FC * 128], wdbf[0], "sd")
            ctx.wait("sync", sem["sst"], c_st)
            if it < B * NLB:
                c_xr = None
                for ec in range(EC):
                    c_xr = dma(ctx, "sync", yT[:, ec * LB:(ec + 1) * LB],
                               x1T[nb2, ec * 128:(ec + 1) * 128, nlb * LB:(nlb + 1) * LB], "sd")
                evac["c_xr"] = c_xr



def stage_C(ctx, st_, nc, PS, x2T, owT, cparams, x3T, x3bf):
    sem = ctx.sems
    gw7h, cw7h, gb8h, cb8h, ld8h, ob8h, nw8h = cparams
    E = st_.enter_context
    XW = S + 8
    owt = E(nc.sbuf_tensor("c_owt", [128, EC * D], BF16))
    xc = E(nc.sbuf_tensor("c_xc", [128, 2 * XW], F32))
    ga = E(nc.sbuf_tensor("c_ga", [128, S], F32))
    gs = E(nc.sbuf_tensor("c_gs", [128, S], F32))
    ct = E(nc.sbuf_tensor("c_ct", [128, S], F32))
    tp = E(nc.sbuf_tensor("c_tp", [128, 2 * S], F32))
    db = E(nc.sbuf_tensor("c_db", [128, S], F32))
    hch = E(nc.sbuf_tensor("c_hch", [128, S], F32))
    hb = E(nc.sbuf_tensor("c_hb", [128, 2 * EC * S], BF16))
    yC = E(nc.sbuf_tensor("c_yC", [128, EC * LB], F32))
    xr2 = E(nc.sbuf_tensor("c_xr2", [128, EC * LB], F32))
    gw7 = E(nc.sbuf_tensor("c_gw7", [128, 56], F32))
    cw7 = E(nc.sbuf_tensor("c_cw7", [128, 56], F32))
    gb8 = E(nc.sbuf_tensor("c_gb8", [128, 8], F32))
    cb8 = E(nc.sbuf_tensor("c_cb8", [128, 8], F32))
    ld8 = E(nc.sbuf_tensor("c_ld8", [128, 8], F32))
    ob8 = E(nc.sbuf_tensor("c_ob8", [128, 8], F32))
    nw8 = E(nc.sbuf_tensor("c_nw8", [128, 8], F32))
    dec8 = E(nc.sbuf_tensor("c_dec8", [128, 8], F32))
    sm, ones_cf, ones_r, cst = alloc_ln(nc, E, "c")
    cstR = E(nc.sbuf_tensor("c_cstR", [1, 1], F32))
    ln_const_init(ctx, ones_cf, ones_r, cst, LN_EPS)
    vop(ctx, "memset", cstR[:], RMS_EPS)

    stgb3 = E(nc.sbuf_tensor("c_stgb3", [128, 2 * LB], BF16))
    c_ow = None
    for ec in range(EC):
        ctx.wait("sync", sem["dve"], c_ow)
        c_l = dma(ctx, "sync", xc[:, 0:D], owT[ec * 128:(ec + 1) * 128, :], "sd")
        ctx.wait("vector", sem["sd"], c_l)
        c_ow = vop(ctx, "tensor_copy", owt[:, ec * D:(ec + 1) * D], xc[:, 0:D], inc=True)
    evac_ow = c_ow
    c_sm = None
    for dst, src in ((gw7, gw7h), (cw7, cw7h), (gb8, gb8h), (cb8, cb8h),
                     (ld8, ld8h), (ob8, ob8h), (nw8, nw8h)):
        c_sm = dma(ctx, "sync", dst[:], src, "sd")
    ctx.wait("scalar", sem["sd"], c_sm)
    c_dec = act(ctx, dec8[:], ld8[:], AF.Exp, inc=True)

    vop(ctx, "memset", xc[:, 0:3], 0.0)
    vop(ctx, "memset", xc[:, S + 3:XW], 0.0)
    vop(ctx, "memset", xc[:, XW:XW + 3], 0.0)
    c_pads = vop(ctx, "memset", xc[:, XW + S + 3:2 * XW], 0.0, inc=True)
    ctx.wait("scalar", sem["dve"], c_pads)

    evac = {}
    evac["xcd_init"] = c_pads

    def conv_chunk(b, ec):
        hp = b % 2
        xp = (b * EC + ec) % 2
        xb_ = xp * XW
        ctx.wait("sync", sem["dve"], evac.get(("xcd", xp)))
        ctx.wait("sync", sem["dve"], evac.get("xcd_init"))
        ctx.wait("sync", sem["act"], evac.get(("xca", xp)))
        c_x = dma(ctx, "sync", xc[:, xb_ + 3:xb_ + 3 + S], x2T[b, ec * 128:(ec + 1) * 128, :], "sd")
        ctx.wait("scalar", sem["sd"], c_x)
        ctx.wait("scalar", sem["dve"], evac.get("gated"))
        ctx.wait("scalar", sem["act"], c_dec)
        cp = None
        for k in range(7):
            tph = k % 2
            ctx.wait("scalar", sem["pool"], evac.get(("tp", tph)))
            ck = act(ctx, tp[:, tph * S:(tph + 1) * S], xc[:, xb_ + k:xb_ + k + S],
                     AF.Identity, scale=cw7[:, ec * 7 + k:ec * 7 + k + 1], inc=True)
            if k == 6:
                evac[("xca", xp)] = ck
            ctx.wait("gpsimd", sem["act"], ck)
            if k == 0:
                ctx.wait("gpsimd", sem["act"], evac.get("tanh"))
                cp = vop(ctx, "tensor_copy", ct[:], tp[:, 0:S], eng="gpsimd", inc=True)
            else:
                cp = vop(ctx, "tensor_tensor", ct[:], ct[:], tp[:, tph * S:tph * S + S],
                         ALU.add, eng="gpsimd", inc=True)
            evac[("tp", tph)] = cp
        ctx.wait("vector", sem["sd"], c_x)
        ctx.wait("vector", sem["act"], evac.get("sig"))
        vop(ctx, "tensor_scalar", ga[:], xc[:, xb_:xb_ + S], gw7[:, ec * 7:ec * 7 + 1], None, ALU.mult)
        cga = None
        for k in range(1, 7):
            cga = vop(ctx, "scalar_tensor_tensor", ga[:], xc[:, xb_ + k:xb_ + k + S],
                      gw7[:, ec * 7 + k:ec * 7 + k + 1], ga[:], ALU.mult, ALU.add,
                      inc=(k == 6))
        evac[("xcd", xp)] = cga
        ctx.wait("scalar", sem["dve"], cga)
        ctx.wait("scalar", sem["dve"], evac.get("gated"))
        c_sg = act(ctx, gs[:], ga[:], AF.Sigmoid, bias=gb8[:, ec:ec + 1], inc=True)
        evac["sig"] = c_sg
        ctx.wait("scalar", sem["pool"], cp)
        c_th = act(ctx, tp[:, 0:S], ct[:], AF.Tanh, bias=cb8[:, ec:ec + 1], inc=True)
        evac["tanh"] = c_th
        ctx.wait("scalar", sem["dve"], evac.get("scan"))
        c_db = act(ctx, db[:], gs[:], AF.Relu, bias=dec8[:, ec:ec + 1], scale=0.0, inc=True)
        ctx.wait("vector", sem["act"], c_th)
        c_gt = vop(ctx, "tensor_tensor", ga[:], gs[:], tp[:, 0:S], ALU.mult, inc=True)
        evac["gated"] = c_gt
        ctx.wait("vector", sem["act"], c_db)
        ctx.wait("vector", sem["pe"], evac.get(("hbfree", hp)))
        c_scan = vop(ctx, "tensor_tensor_scan", hch[:], db[:], ga[:], 0.0, ALU.mult, ALU.add,
                     inc=True)
        evac["scan"] = c_scan
        c_hb = vop(ctx, "tensor_copy", hb[:, hp * EC * S + ec * S:hp * EC * S + (ec + 1) * S],
                   hch[:], inc=True)
        evac[("hbready", hp, ec)] = c_hb

    def norm_lb(b, lb):
        hp = b % 2
        hbase = hp * EC * S
        ctx.wait("sync", sem["dve"], evac.get("xr2"))
        c_y = None
        for ec in range(EC):
            c_y = dma(ctx, "sync", xr2[:, ec * LB:(ec + 1) * LB],
                      x2T[b, ec * 128:(ec + 1) * 128, lb * LB:(lb + 1) * LB], "sd")
        ctx.wait("tensor", sem["dve"], evac.get(("hbready", hp, EC - 1)))
        cm = None
        for et in range(EC):
            wb = et % 2
            ctx.wait("tensor", sem["dve"], evac.get(wb))
            if evac.get("c_ow_waited") is None:
                ctx.wait("tensor", sem["dve"], evac_ow)
                evac["c_ow_waited"] = 1
            for dc in range(EC):
                cm = mm(ctx, PS[:, wb * LB:(wb + 1) * LB],
                        owt[:, dc * D + et * 128:dc * D + et * 128 + 128],
                        hb[:, hbase + dc * S + lb * LB:hbase + dc * S + lb * LB + LB],
                        start=(dc == 0), stop=(dc == EC - 1), fence=(dc == EC - 1))
            ctx.wait("vector", sem["pe"], cm)
            if et == 0:
                ctx.wait("vector", sem["sd"], c_y)
                ctx.wait("vector", sem["sst"], evac.get("yCst"))
            evac[wb] = vop(ctx, "scalar_tensor_tensor", yC[:, et * LB:(et + 1) * LB],
                           PS[:, wb * LB:(wb + 1) * LB], ob8[:, et:et + 1],
                           xr2[:, et * LB:(et + 1) * LB], ALU.add, ALU.add, inc=True)
        evac[("norm_mm", b)] = cm
        ln_apply(ctx, PS, evac, yC, sm, ones_cf, ones_r, cstR[0:1, 0:1], rms=True, nw=nw8)
        last = None
        for ec in range(EC):
            last = vop(ctx, "tensor_tensor", yC[:, ec * LB:(ec + 1) * LB],
                       yC[:, ec * LB:(ec + 1) * LB], xr2[:, ec * LB:(ec + 1) * LB],
                       ALU.add, inc=(ec == EC - 1))
        evac["xr2"] = last
        ln_apply(ctx, PS, evac, yC, sm, ones_cf, ones_r, cst[0:1, 0:1])
        c_st = None
        for ec in range(EC):
            hb2 = ec % 2
            ctx.wait("sync", sem["dve"], evac[("ln", ec)])
            c_st = dma(ctx, "sync", x3T[b, ec * 128:(ec + 1) * 128, lb * LB:(lb + 1) * LB],
                       yC[:, ec * LB:(ec + 1) * LB], "sst")
            ctx.wait("vector", sem["sst"], evac.get(("mir3", hb2)))
            c_mc = vop(ctx, "tensor_copy", stgb3[:, hb2 * LB:(hb2 + 1) * LB],
                       yC[:, ec * LB:(ec + 1) * LB], inc=True)
            ctx.wait("sync", sem["dve"], c_mc)
            c_st = dma(ctx, "sync", x3bf[b, ec * 128:(ec + 1) * 128, lb * LB:(lb + 1) * LB],
                       stgb3[:, hb2 * LB:(hb2 + 1) * LB], "sst")
            evac[("mir3", hb2)] = c_st
        evac["yCst"] = c_st

    for ec in range(EC):
        conv_chunk(0, ec)
    for b in range(B):
        for lb in range(NLB):
            norm_lb(b, lb)
            if b + 1 < B:
                conv_chunk(b + 1, 2 * lb)
                conv_chunk(b + 1, 2 * lb + 1)
        evac[("hbfree", b % 2)] = evac[("norm_mm", b)]


def stage_D(ctx, st_, nc, PS, x3T, x3bf, wlT, alphah, blh, outT):
    sem = ctx.sems
    E = st_.enter_context
    wl = E(nc.sbuf_tensor("d_wl", [128, EC * D], F32))
    w3 = E(nc.sbuf_tensor("d_w3", [128, EC * D], BF16))
    gtile = E(nc.sbuf_tensor("d_g", [128, D], F32))
    ltile = E(nc.sbuf_tensor("d_l", [128, D], F32))
    wabs = E(nc.sbuf_tensor("d_wabs", [128, 8], F32))
    delt = E(nc.sbuf_tensor("d_delt", [128, 1], F32))
    ndelt = E(nc.sbuf_tensor("d_ndelt", [128, 1], F32))
    alph = E(nc.sbuf_tensor("d_alph", [128, 1], F32))
    alps = E(nc.sbuf_tensor("d_alps", [1, 1], F32))
    bl = E(nc.sbuf_tensor("d_bl", [128, 8], F32))
    x3b = E(nc.sbuf_tensor("d_x3b", [128, 2 * EC * LB], BF16))
    yT = E(nc.sbuf_tensor("d_yT", [128, EC * LB], F32))
    tmpc = E(nc.sbuf_tensor("d_tmpc", [128, LB], F32))
    sm, ones_cf, ones_r, cst = alloc_ln(nc, E, "d")
    ln_const_init(ctx, ones_cf, ones_r, cst, LN_EPS)

    c_wl = dma(ctx, "sync", wl[:].rearrange("p (c n) -> p c n", n=D),
               wlT.rearrange("(c p) n -> p c n", p=128), "sd")
    dma(ctx, "sync", alps[:], alphah, "sd")
    c_bl = dma(ctx, "sync", bl[:], blh, "sd")
    ctx.wait("vector", sem["sd"], c_wl)
    c_ab = None
    for ec in range(EC):
        c_ab = vop(ctx, "tensor_reduce", wabs[:, ec:ec + 1], wl[:, ec * D:(ec + 1) * D],
                   AX.X, ALU.add, inc=(ec == EC - 1), apply_absolute_value=True)
    ctx.wait("tensor", sem["dve"], c_ab)
    c_mm = mm(ctx, PS[0:1, 0:8], ones_cf[:, 0:1], wabs[:], start=True, stop=True, fence=True,
              fence_ap=PS[0:1, 0:256])
    ctx.wait("vector", sem["pe"], c_mm)
    vop(ctx, "tensor_reduce", sm["m"][0:1, 0:1], PS[0:1, 0:8], AX.X, ALU.add, selfsync=True)
    c_d = vop(ctx, "tensor_scalar", sm["m"][0:1, 0:1], sm["m"][0:1, 0:1],
              0.7 / (D * D), None, ALU.mult, inc=True, selfsync=True)
    ctx.wait("tensor", sem["dve"], c_d)
    ctx.wait("tensor", sem["sd"], c_bl)
    mm(ctx, PS[0:128, LB:LB + 1], ones_r[0:1, 0:128], sm["m"][0:1, 0:1], start=True, stop=True)
    c_mm2 = mm(ctx, PS[0:128, 2 * LB:2 * LB + 1], ones_r[0:1, 0:128], alps[:], start=True, stop=True,
               fence=True, fence_ap=PS[0:1, 2 * LB:2 * LB + 256])
    ctx.wait("vector", sem["pe"], c_mm2)
    vop(ctx, "tensor_copy", delt[:], PS[0:128, LB:LB + 1])
    vop(ctx, "tensor_copy", alph[:], PS[0:128, 2 * LB:2 * LB + 1], selfsync=True)
    vop(ctx, "tensor_scalar", ndelt[:], delt[:], -1.0, None, ALU.mult, selfsync=True)
    c_tw = None
    for ec in range(EC):
        vop(ctx, "tensor_scalar", gtile[:], wl[:, ec * D:(ec + 1) * D], delt[:, 0:1], None, ALU.is_gt)
        vop(ctx, "tensor_scalar", ltile[:], wl[:, ec * D:(ec + 1) * D], ndelt[:, 0:1], None, ALU.is_lt)
        c_tw = vop(ctx, "tensor_tensor", w3[:, ec * D:(ec + 1) * D], gtile[:], ltile[:],
                   ALU.subtract, inc=(ec == EC - 1))
    ctx.wait("tensor", sem["dve"], c_tw)

    evac = {}
    c_xr = None
    for ec in range(EC):
        c_xr = dma(ctx, "sync", yT[:, ec * LB:(ec + 1) * LB],
                   x3T[0, ec * 128:(ec + 1) * 128, 0:LB], "sd")
    evac["c_xr"] = c_xr
    it = 0
    for b in range(B):
        for lb in range(NLB):
            xp = it % 2
            xbase = xp * EC * LB
            c_x = evac.get("c_x_pref")
            if c_x is None:
                ctx.wait("sync", sem["pe"], evac.get(("x3b", xp)))
                for ec in range(EC):
                    c_x = dma(ctx, "sync", x3b[:, xbase + ec * LB:xbase + (ec + 1) * LB],
                              x3bf[b, ec * 128:(ec + 1) * 128, lb * LB:(lb + 1) * LB], "sd")
            ctx.wait("tensor", sem["sd"], c_x)
            cm = None
            for ot in range(EC):
                wb = ot % 2
                ctx.wait("tensor", sem["dve"], evac.get(wb))
                for ec in range(EC):
                    cm = mm(ctx, PS[:, wb * LB:(wb + 1) * LB],
                            w3[:, ec * D + ot * 128:ec * D + ot * 128 + 128],
                            x3b[:, xbase + ec * LB:xbase + (ec + 1) * LB],
                            start=(ec == 0), stop=(ec == EC - 1), fence=(ec == EC - 1))
                ctx.wait("vector", sem["pe"], cm)
                if ot == 0:
                    ctx.wait("vector", sem["sd"], evac.get("c_xr"))
                vop(ctx, "tensor_scalar", tmpc[:], PS[:, wb * LB:(wb + 1) * LB],
                    alph[:, 0:1], bl[:, ot:ot + 1], ALU.mult, ALU.add)
                evac[wb] = vop(ctx, "tensor_tensor", yT[:, ot * LB:(ot + 1) * LB],
                               yT[:, ot * LB:(ot + 1) * LB], tmpc[:], ALU.add, inc=True)
            evac[("x3b", xp)] = cm
            ln_apply(ctx, PS, evac, yT, sm, ones_cf, ones_r, cst[0:1, 0:1],
                     sum_banks=(2, 3), bc_banks=(4, 5))
            c_st = None
            for ec in range(EC):
                ctx.wait("sync", sem["dve"], evac[("ln", ec)])
                c_st = dma(ctx, "sync", outT[b, ec * 128:(ec + 1) * 128, lb * LB:(lb + 1) * LB],
                           yT[:, ec * LB:(ec + 1) * LB], "sst")
            it += 1
            if it < B * NLB:
                nb2, nlb = divmod(it, NLB)
                nxp = it % 2
                nxbase = nxp * EC * LB
                ctx.wait("sync", sem["pe"], evac.get(("x3b", nxp)))
                c_xp = None
                for ec in range(EC):
                    c_xp = dma(ctx, "sync", x3b[:, nxbase + ec * LB:nxbase + (ec + 1) * LB],
                               x3bf[nb2, ec * 128:(ec + 1) * 128, nlb * LB:(nlb + 1) * LB], "sd")
                evac["c_x_pref"] = c_xp
            ctx.wait("sync", sem["sst"], c_st)
            if it < B * NLB:
                c_xr = None
                for ec in range(EC):
                    c_xr = dma(ctx, "sync", yT[:, ec * LB:(ec + 1) * LB],
                               x3T[nb2, ec * 128:(ec + 1) * 128, nlb * LB:(nlb + 1) * LB], "sd")
                evac["c_xr"] = c_xr



def build_program(stage_names):
    nc = bass.Bass("TRN2", target_bir_lowering=False)
    ctx = Ctx(nc)
    last = stage_names[-1]
    d = {}
    kinds = {}

    def dram(name, shape, dtype, kind):
        d[name] = nc.dram_tensor(name, shape, dtype, kind=kind).ap()
        kinds[name] = kind
        return d[name]

    def io(name, produced_by, consumed_by):
        if produced_by in stage_names and consumed_by in stage_names:
            kind = "ExternalOutput" if DEBUG_IO else "Internal"
        elif produced_by in stage_names:
            kind = "ExternalOutput"
        else:
            kind = "ExternalInput"
        return dram(name, [B, D, S], F32, kind)

    def iobf(name, produced_by, consumed_by, shape):
        if produced_by in stage_names:
            kind = "ExternalOutput"
        else:
            kind = "ExternalInput"
        return dram(name, shape, BF16, kind)

    if "A" in stage_names:
        dram("xT", [B, D, S], F32, "ExternalInput")
        for n in ("wq2", "wk2", "wv2", "wo2"):
            dram(n, [D, D], F32, "ExternalInput")
    if "A" in stage_names or "B" in stage_names:
        io("x1T", "A", "B")
        iobf("x1bf", "A", "B", [B, D, S])
    if "B" in stage_names:
        dram("wgT", [D, FF], I32, "ExternalInput")
        dram("wuT", [D, FF], I32, "ExternalInput")
        dram("wdH", [EC, 128, FC * 128], I32, "ExternalInput")
        dram("wdbf", [EC, 128, FC * 128], BF16, "ExternalOutput")
    if "B" in stage_names or "C" in stage_names:
        io("x2T", "B", "C")
    if "C" in stage_names:
        dram("owT", [D, D], F32, "ExternalInput")
        dram("gw7", [128, 56], F32, "ExternalInput")
        dram("cw7", [128, 56], F32, "ExternalInput")
        for n in ("gb8", "cb8", "ld8", "ob8", "nw8"):
            dram(n, [128, 8], F32, "ExternalInput")
    if "C" in stage_names or "D" in stage_names:
        io("x3T", "C", "D")
        iobf("x3bf", "C", "D", [B, D, S])
    if "D" in stage_names:
        dram("wlT", [D, D], F32, "ExternalInput")
        dram("alphah", [1, 1], F32, "ExternalInput")
        dram("blh", [128, 8], F32, "ExternalInput")
        dram("outT", [B, D, S], F32, "ExternalOutput")

    with ExitStack() as outer:
        ctx.make_sems(outer)
        PS = outer.enter_context(nc.psum_tensor("PS", [128, 4096], F32))
        ctx.zf = outer.enter_context(nc.sbuf_tensor("zfence", [1, 256], BF16))
        c_zf = vop(ctx, "memset", ctx.zf[:], 0.0, inc=True)
        ctx.wait("tensor", ctx.sems["dve"], c_zf)
        block = outer.enter_context(nc.Block())
        for sname in stage_names:
            with ExitStack() as st_:
                if sname == "A":
                    stage_A(ctx, st_, nc, PS, d["xT"], d["wq2"], d["wk2"], d["wv2"],
                            d["wo2"], d["x1T"], d["x1bf"])
                elif sname == "B":
                    stage_B(ctx, st_, nc, PS, d["x1T"], d["x1bf"], d["wgT"], d["wuT"],
                            d["wdH"], d["wdbf"], d["x2T"])
                elif sname == "C":
                    cparams = (d["gw7"], d["cw7"], d["gb8"], d["cb8"], d["ld8"],
                               d["ob8"], d["nw8"])
                    stage_C(ctx, st_, nc, PS, d["x2T"], d["owT"], cparams, d["x3T"],
                            d["x3bf"])
                elif sname == "D":
                    stage_D(ctx, st_, nc, PS, d["x3T"], d["x3bf"], d["wlT"], d["alphah"],
                            d["blh"], d["outT"])
                else:
                    raise ValueError(sname)
                if sname != last:
                    ctx.barrier()
                else:
                    ctx.wait("sync", ctx.sems["sst"], ctx.sems["sst"].n)
                    ctx.wait("sync", ctx.sems["sd"], ctx.sems["sd"].n)
                    ctx.wait("gpsimd", ctx.sems["sg"], ctx.sems["sg"].n)
                ctx.replay(block)
    return nc, d, kinds



def prep_attn_weights(W_q, W_k, W_v, W_o):
    wq2 = np.ascontiguousarray(np.transpose(np.asarray(W_q), (1, 0, 2)).reshape(D, D))
    wk2 = np.ascontiguousarray(np.transpose(np.asarray(W_k), (1, 0, 2)).reshape(D, D))
    wv2 = np.ascontiguousarray(np.transpose(np.asarray(W_v), (1, 0, 2)).reshape(D, D))
    wo2 = np.ascontiguousarray(np.asarray(W_o))
    return wq2, wk2, wv2, wo2


def xT_of(x):
    return np.ascontiguousarray(np.transpose(np.asarray(x), (0, 2, 1)))


def vec8(v):
    return np.ascontiguousarray(np.asarray(v).reshape(8, 128).T)


def prep_all(inputs):
    wq2, wk2, wv2, wo2 = prep_attn_weights(inputs["L0_attn_W_q"], inputs["L0_attn_W_k"],
                                           inputs["L0_attn_W_v"], inputs["L0_attn_W_o"])
    m = {
        "wq2": wq2, "wk2": wk2, "wv2": wv2, "wo2": wo2,
        "wgT": np.ascontiguousarray(np.asarray(inputs["L0_ffn_w_gate"]).T),
        "wuT": np.ascontiguousarray(np.asarray(inputs["L0_ffn_w_up"]).T),
        "wdH": np.ascontiguousarray(
            np.asarray(inputs["L0_ffn_w_down"]).T.reshape(FC, 128, EC, 128)
            .transpose(2, 1, 0, 3).reshape(EC, 128, FC * 128)),
        "owT": np.ascontiguousarray(np.asarray(inputs["L1_attn_out_w"]).T),
        "gw7": np.ascontiguousarray(
            np.asarray(inputs["L1_attn_gate_w"])[:, 0, :].reshape(8, 128, 7)
            .transpose(1, 0, 2).reshape(128, 56)),
        "cw7": np.ascontiguousarray(
            np.asarray(inputs["L1_attn_cand_w"])[:, 0, :].reshape(8, 128, 7)
            .transpose(1, 0, 2).reshape(128, 56)),
        "gb8": vec8(inputs["L1_attn_gate_b"]),
        "cb8": vec8(inputs["L1_attn_cand_b"]),
        "ld8": vec8(inputs["L1_attn_log_decay"]),
        "ob8": vec8(inputs["L1_attn_out_b"]),
        "nw8": vec8(inputs["L1_attn_norm_w"]),
        "wlT": np.ascontiguousarray(np.asarray(inputs["L1_ffn_weight"]).T),
        "alphah": np.asarray(inputs["L1_ffn_alpha"], dtype=np.float32).reshape(1, 1),
        "blh": vec8(inputs["L1_ffn_bias"]),
    }
    return m


STAGE_GROUPS = [["A", "B", "C", "D"]]


def kernel(**inputs):
    shared = prep_all(inputs)
    x = np.asarray(inputs["L0_attn_x"])
    xTs = [xT_of(x[c * B:(c + 1) * B]) for c in range(NCORES)]

    per_core_state = [dict() for _ in range(NCORES)]
    for c in range(NCORES):
        per_core_state[c]["xT"] = xTs[c]

    for gi, group in enumerate(STAGE_GROUPS):
        nc, d, kinds = build_program(group)
        in_maps = []
        for c in range(NCORES):
            im = {k: (shared[k] if k in shared else per_core_state[c][k])
                  for k, kind in kinds.items() if kind == "ExternalInput"}
            in_maps.append(im)
        res = run_bass_kernel_spmd(nc, in_maps, core_ids=list(range(NCORES)))
        for c in range(NCORES):
            for k, v in res.results[c].items():
                per_core_state[c][k] = v
    outs = [np.transpose(per_core_state[c]["outT"], (0, 2, 1)) for c in range(NCORES)]
    return np.ascontiguousarray(np.concatenate(outs, axis=0), dtype=np.float32)
